# version 23
# baseline (speedup 1.0000x reference)
import numpy as np
import ml_dtypes
from contextlib import ExitStack

import concourse.bass as bass
import concourse.mybir as mybir
import concourse.bass_utils as _bass_utils
from concourse.bass_utils import run_bass_kernel_spmd

if not getattr(_bass_utils, "_ldw_opt_patched", False):
    _orig_run_command = _bass_utils.run_command

    def _run_command_ldw(cmd, *a, **k):
        cmd = ["--enable-ldw-opt=true" if c == "--enable-ldw-opt=false" else c
               for c in cmd]
        return _orig_run_command(cmd, *a, **k)

    _bass_utils.run_command = _run_command_ldw
    _bass_utils._ldw_opt_patched = True

F32 = mybir.dt.float32
I32 = mybir.dt.int32
BF16 = mybir.dt.bfloat16
FP8 = mybir.dt.float8e4
AF = mybir.ActivationFunctionType
ALU = mybir.AluOpType
AX = mybir.AxisListType

C = 1024
H = 16
TILE = 128
NT = 16
TOK = TILE * NT
EPS_LN = 1e-5
EPS_GN = 1e-5 * 64
MAGIC = float(np.float32(1.5 * 2 ** 23))
ECLAMP = 1e30

ENGS = ("sync", "tensor", "vector", "scalar", "gpsimd")


class Prog:

    def __init__(self):
        self.stream = {e: [] for e in ENGS}
        self.count = {}
        self.bufw = {}
        self.bufr = {}
        self.hwm = {}
        self.defer = None
        self.dma_pools = {}

    def chunk(self):
        if self.defer is not None:
            self.defer.append(None)

    def staged(self, *fns):
        lists = []
        for f in fns:
            self.defer = []
            f()
            chunks, cur = [], []
            for it in self.defer:
                if it is None:
                    if cur:
                        chunks.append(cur)
                    cur = []
                else:
                    cur.append(it)
            if cur:
                chunks.append(cur)
            lists.append(chunks)
        self.defer = None
        ccost = [[sum(it[8] for it in c) for c in l] for l in lists]
        nops = [sum(cl) for cl in ccost]
        done = [0.0] * len(lists)
        idx = [0] * len(lists)
        while True:
            best, bf = -1, 2.0
            for i, l in enumerate(lists):
                if idx[i] < len(l):
                    frac = done[i] / max(nops[i], 1e-9)
                    if frac < bf:
                        best, bf = i, frac
            if best < 0:
                break
            for it in lists[best][idx[best]]:
                self._emit(*it[:8])
            done[best] += ccost[best][idx[best]]
            idx[best] += 1

    DEF_COST = {"vector": 0.7, "scalar": 0.9, "tensor": 0.28, "gpsimd": 1.5,
                "sync": 0.3}

    def op(self, eng, fn, sem, inc_n=1, reads=(), writes=(), waits=(),
           dma_kind=None, cost=None):
        if cost is None:
            cost = self.DEF_COST[eng]
        if self.defer is not None:
            self.defer.append((eng, fn, sem, inc_n, reads, writes, waits,
                               dma_kind, cost))
            return None
        return self._emit(eng, fn, sem, inc_n, reads, writes, waits, dma_kind)

    def _emit(self, eng, fn, sem, inc_n, reads, writes, waits, dma_kind):
        if dma_kind is not None:
            pool, cnt = self.dma_pools[dma_kind]
            sem = pool[cnt[0] % len(pool)]
            cnt[0] += 1
            inc_n = 16
            waits = tuple(waits) + ((sem, self.count.get(sem, 0)),)
        hwm = self.hwm.setdefault(eng, {})
        wl = {}
        for t in waits:
            if t is not None:
                wl[t[0]] = max(wl.get(t[0], 0), t[1])
        for b in reads:
            t = self.bufw.get(b)
            if t is not None:
                wl[t[0]] = max(wl.get(t[0], 0), t[1])
        for b in writes:
            for t in self.bufr.get(b, ()):
                wl[t[0]] = max(wl.get(t[0], 0), t[1])
            t = self.bufw.get(b)
            if t is not None:
                wl[t[0]] = max(wl.get(t[0], 0), t[1])
        self.count[sem] = self.count.get(sem, 0) + inc_n
        tok = (sem, self.count[sem])
        wl2 = []
        for s, v in wl.items():
            if hwm.get(s, -1) >= v:
                continue
            hwm[s] = v
            wl2.append((s, v))
        self.stream[eng].append((fn, tuple(wl2), sem, inc_n))
        for b in reads:
            self.bufr.setdefault(b, []).append(tok)
        for b in writes:
            self.bufw[b] = tok
            self.bufr[b] = []
        return tok

    def barrier_wait(self, eng, waits):
        self.stream[eng].append((None, tuple(waits), None, 0))

    def runner(self, eng_name):
        stream = self.stream[eng_name]

        def run(eng):
            for fn, waits, sem, inc_n in stream:
                for s, val in waits:
                    eng.wait_ge(s, val)
                if fn is None:
                    continue
                inst = fn(eng)
                if sem is not None:
                    inst.then_inc(sem, inc_n)
        return run


def build_nc(debug_names=(), n_tiles=NT, phase2=True, p2_tiles=None):
    nc = bass.Bass("TRN2", target_bir_lowering=False)

    xin = nc.dram_tensor("x", [TOK + 1, C], F32, kind="ExternalInput").ap()
    wdr = {}
    for nm in ("wkT", "wvT", "wrT", "wgT", "woT"):
        wdr[nm] = nc.dram_tensor(nm, [C, C], F32, kind="ExternalInput").ap()
    wffn = nc.dram_tensor("wffn", [C, C], F32, kind="ExternalInput").ap()
    maa4 = nc.dram_tensor("maa4", [128, 32], F32, kind="ExternalInput").ap()
    decay_bd = nc.dram_tensor("decay_bd", [128, C], F32, kind="ExternalInput").ap()
    decay_hd = nc.dram_tensor("decay_hd", [128, C], F32, kind="ExternalInput").ap()
    qpat = nc.dram_tensor("qpat", [128, C], F32, kind="ExternalInput").ap()
    imj = nc.dram_tensor("imj", [128, 2 * C], F32, kind="ExternalInput").ap()
    maskio = nc.dram_tensor("maskio", [128, 2 * C], F32, kind="ExternalInput").ap()
    udiag = nc.dram_tensor("udiag", [128, 2 * C], F32, kind="ExternalInput").ap()
    decay_wm = nc.dram_tensor("decay_wm", [128, 2 * C], F32, kind="ExternalInput").ap()
    iotarev = nc.dram_tensor("iotarev", [128, 1], F32, kind="ExternalInput").ap()
    lnw_bc = nc.dram_tensor("lnw_bc", [128, C], F32, kind="ExternalInput").ap()
    lnb_bc = nc.dram_tensor("lnb_bc", [128, C], F32, kind="ExternalInput").ap()
    bias_bc = nc.dram_tensor("bias_bc", [128, C], F32, kind="ExternalInput").ap()
    idin = nc.dram_tensor("idin", [128, 128], BF16, kind="ExternalInput").ap()
    relay_d = nc.dram_tensor("relay_d", [128, 4], F32, kind="ExternalInput").ap()
    relay_m = nc.dram_tensor("relay_m", [128, 4], F32, kind="ExternalInput").ap()
    yout = nc.dram_tensor("y", [TOK, C], F32, kind="ExternalOutput").ap()
    cc_in = nc.dram_tensor("cc_in", [128, C], F32)
    cc_out = nc.dram_tensor("cc_out", [512, C], F32)
    dbg_dram = {}
    for nm, shp in debug_names:
        dbg_dram[nm] = nc.dram_tensor("dbg_" + nm, list(shp), F32,
                                      kind="ExternalOutput").ap()

    p = Prog()

    with ExitStack() as stck:
        en = stck.enter_context

        def sb(name, shape, dtype):
            return en(nc.sbuf_tensor(name, shape, dtype))

        WA = sb("WA", [128, 8 * C], BF16)
        WB = sb("WB", [128, 8 * C], BF16)
        WC = sb("WC", [128, 8 * C], BF16)
        WD = sb("WD", [128, 8 * C], BF16)
        WE = sb("WE", [128, 8 * C], BF16)
        WF = sb("WF", [128, 8 * C], BF16)
        S = sb("S", [128, C], F32)

        maa_sb = sb("maa_sb", [128, 32], F32)
        wk_tm = sb("wk_tm", [128, C], BF16)
        wb_all = sb("wb_all", [128, C], BF16)
        ws_pair = sb("ws_pair", [128, 8], F32)
        wm_b = sb("wm_b", [128, 2 * C], BF16)
        lnw_sb = sb("lnw_sb", [128, C], BF16)
        lnb_sb = sb("lnb_sb", [128, C], BF16)
        bias_sb = sb("bias_sb", [128, C], BF16)
        ident = sb("ident", [128, 128], BF16)
        ones1 = sb("ones1", [1, 128], BF16)
        relay_d_sb = sb("relay_d_sb", [128, 4], F32)
        relay_m_sb = sb("relay_m_sb", [128, 4], F32)
        fac4 = sb("fac4", [128, 32], F32)
        halo = sb("halo", [128, 8], BF16)
        halo0 = sb("halo0", [128, 8], BF16)
        m1 = sb("m1", [128, NT], F32)
        rstd1 = sb("rstd1", [128, NT], F32)
        nmr1 = sb("nmr1", [128, NT], F32)

        class Bufs:
            pass

        BB = []
        for par in range(2):
            b = Bufs()
            b.p = str(par)
            for nm in ("ln_tm", "d_fm", "mixA", "mixB", "k_tm", "v_tm",
                       "k_fm", "r_fm", "rwb", "g_tm", "z1"):
                setattr(b, nm, sb(nm + b.p, [128, C], BF16))
            for nm in ("xbuf", "x2"):
                setattr(b, nm, sb(nm + b.p, [128, C], F32))
            for nm in ("stA", "stW", "stX", "stF"):
                setattr(b, nm, sb(nm + b.p, [128, 8], F32))
            BB.append(b)
        shared_gn = {nm: sb(nm + "_s", [128, H], F32)
                     for nm in ("gn_m", "gn_r", "gsum", "gsq", "gs1")}
        shared_q = {nm: sb(nm + "_s", [128, 32], F32)
                    for nm in ("qs", "qf", "qbm")}
        for b in BB:
            for nm, tt in {**shared_gn, **shared_q}.items():
                setattr(b, nm, tt)
        shared_bf = {nm: sb(nm + "_s", [128, C], BF16)
                     for nm in ("attW", "z_fm", "y_bf", "kwk", "ln_fm",
                                "xd", "xd_fm")}
        shared_f32 = {nm: sb(nm + "_s", [128, C], F32)
                      for nm in ("junkA", "junkB", "ff_tm", "t0")}
        q8_s = sb("q8_s", [128, C], FP8)
        for b in BB:
            for nm, tt in shared_bf.items():
                setattr(b, nm, tt)
            for nm, tt in shared_f32.items():
                setattr(b, nm, tt)
            b.q8 = q8_s
            b.S_bf = b.d_fm
            b.G1 = b.mixA
            b.G2 = b.mixB
            b.ao = b.t0
            b.vq = b.junkA
            b.r_tm = b.ln_tm
            b.ysq = b.attW

        SHARED = ("attW", "z_fm", "y_bf", "kwk", "ln_fm", "xd", "xd_fm",
                  "junkA", "junkB", "ff_tm", "t0", "q8",
                  "gn_m", "gn_r", "gsum", "gsq", "gs1", "qs", "qf", "qbm")

        e_bd = BB[0].x2

        def N(b, nm):
            if nm in SHARED:
                return nm + "_s"
            return nm + b.p

        PPa = en(nc.psum_tensor("PPa", [128, 512], F32))
        PPb = en(nc.psum_tensor("PPb", [128, 512], F32))
        PAe = en(nc.psum_tensor("PAe", [128, 512], F32))
        PAo = en(nc.psum_tensor("PAo", [128, 512], F32))
        PY = en(nc.psum_tensor("PY", [128, 512], F32))
        PY2 = en(nc.psum_tensor("PY2", [128, 512], F32))
        PT = en(nc.psum_tensor("PT", [128, C], BF16))
        PT2 = en(nc.psum_tensor("PT2", [128, C], BF16))

        SINS = [en(nc.semaphore(f"sIN{i}")) for i in range(8)]
        SPDS = [en(nc.semaphore(f"sPD{i}")) for i in range(4)]
        SOUTS = [en(nc.semaphore(f"sOUT{i}")) for i in range(2)]
        sPE = en(nc.semaphore("sPE"))
        sDV = en(nc.semaphore("sDV"))
        sAC = en(nc.semaphore("sAC"))
        sPO = en(nc.semaphore("sPO"))
        sCC = en(nc.semaphore("sCC"))

        p.dma_pools = {"in": (SINS, [0]), "pd": (SPDS, [0]),
                       "out": (SOUTS, [0])}

        def _rot_dma(eng, key, dst, src, reads, writes, waits, **kw):
            return p.op(eng,
                        lambda g, a=dst, b=src: g.dma_start(out=a, in_=b,
                                                            **kw),
                        None, 16, reads, writes, waits, dma_kind=key,
                        cost=0.3)

        def dma(dst, src, reads=(), writes=(), waits=()):
            return _rot_dma("sync", "in", dst, src, reads, writes, waits)

        def dma_out(dst, src, reads=(), waits=()):
            return _rot_dma("sync", "out", dst, src, reads, (), waits)

        def pdma(dst, src, reads=(), writes=(), waits=(), **kw):
            return _rot_dma("gpsimd", "pd", dst, src, reads, writes, waits,
                            **kw)

        def dve(fn, reads=(), writes=(), waits=(), cost=None):
            return p.op("vector", fn, sDV, 1, reads, writes, waits,
                        cost=cost)

        def act(fn, reads=(), writes=(), waits=(), cost=None):
            return p.op("scalar", fn, sAC, 1, reads, writes, waits,
                        cost=cost)

        def pe(fn, reads=(), writes=(), waits=(), cost=None):
            return p.op("tensor", fn, sPE, 1, reads, writes, waits,
                        cost=cost)

        def pool(fn, reads=(), writes=(), waits=()):
            return p.op("gpsimd", fn, sPO, 1, reads, writes, waits)

        def dump(name, src_ap, reads):
            if name in dbg_dram:
                _rot_dma("gpsimd", "out", dbg_dram[name], src_ap, reads, (),
                         ())

        def transpose_set(src_sb, src_name, dst_sb, dst_name):
            for cc in range(8):
                P, ptn = ((PT, "PT"), (PT2, "PT2"))[cc % 2]
                pe(lambda t, c=cc, s=src_sb, P=P: t.transpose(
                    P[:, 128 * (c // 2):128 * (c // 2) + 128],
                    s[:, 128 * c:128 * c + 128], ident[:]),
                   reads=(src_name, "ident"), writes=(ptn,))
            dv = dst_sb.rearrange("p (c e q) -> p c e q", e=2, q=128)
            act(lambda s, d=dv: s.copy(
                out=d[:, :, 0, :],
                in_=PT.rearrange("p (c q) -> p c q", q=128)[:, 0:4, :]),
                reads=("PT",), writes=(dst_name,))
            tk = act(lambda s, d=dv: s.copy(
                out=d[:, :, 1, :],
                in_=PT2.rearrange("p (c q) -> p c q", q=128)[:, 0:4, :]),
                reads=("PT2",), writes=(dst_name,))
            p.chunk()
            return tk

        def dve_rsqrt_col(src, tmp, rcol, names_r, names_w):
            dve(lambda v: v.tensor_scalar(
                out=tmp.bitcast(I32), in0=src.bitcast(I32), scalar1=1,
                scalar2=None, op0=ALU.logical_shift_right),
                reads=names_r, writes=names_w, cost=0.2)
            dve(lambda v: v.tensor_scalar(
                out=rcol.bitcast(I32), in0=tmp.bitcast(I32), scalar1=-1,
                scalar2=0x5f3759df, op0=ALU.mult, op1=ALU.add),
                reads=names_r, writes=names_w, cost=0.2)
            dve(lambda v: v.tensor_tensor(out=tmp, in0=rcol, in1=rcol,
                                          op=ALU.mult),
                reads=names_r, writes=names_w, cost=0.2)
            dve(lambda v: v.scalar_tensor_tensor(
                out=tmp, in0=tmp, scalar=-0.5, in1=src, op0=ALU.mult,
                op1=ALU.mult),
                reads=names_r, writes=names_w, cost=0.2)
            dve(lambda v: v.scalar_tensor_tensor(
                out=rcol, in0=tmp, scalar=1.5, in1=rcol, op0=ALU.add,
                op1=ALU.mult),
                reads=names_r, writes=names_w, cost=0.2)

        def ln_stats(b, src_sb, src_name, st, st_name0, mcol, rcol, nmrcol,
                     eps, outname=None, dve_rs=False, jB=None, pre_sum=None):
            st_name = (st_name0, outname) if outname else (st_name0,)
            jb_ap, jb_nm = jB if jB else (b.junkB, N(b, "junkB"))
            if pre_sum is not None:
                dve(lambda v: v.scalar_tensor_tensor(
                    out=st[:, 0:1], in0=pre_sum[0], scalar=1.0,
                    in1=pre_sum[1], op0=ALU.mult, op1=ALU.add),
                    reads=st_name, writes=st_name, cost=0.2)
            else:
                dve(lambda v, s=src_sb: v.tensor_scalar(
                    out=s[:], in0=s[:], scalar1=1.0, scalar2=0.0,
                    op0=ALU.mult, op1=ALU.add, accum_out=st[:, 0:1]),
                    reads=(src_name,), writes=(src_name,) + st_name)
            act(lambda s, q=src_sb: s.activation(
                out=jb_ap[:], in_=q[:], func=AF.Square,
                accum_out=st[:, 1:2]),
                reads=(src_name,), writes=(jb_nm,) + st_name)
            p.chunk()
            dve(lambda v: v.tensor_scalar_mul(out=mcol, in0=st[:, 0:1],
                                              scalar1=1.0 / C),
                reads=st_name, writes=st_name, cost=0.2)
            dve(lambda v: v.tensor_scalar(
                out=st[:, 2:3], in0=mcol, scalar1=mcol, scalar2=float(eps),
                op0=ALU.mult, op1=ALU.subtract),
                reads=st_name, writes=st_name, cost=0.2)
            dve(lambda v: v.tensor_scalar(
                out=st[:, 3:4], in0=st[:, 1:2], scalar1=1.0 / C,
                scalar2=st[:, 2:3], op0=ALU.mult, op1=ALU.subtract),
                reads=st_name, writes=st_name, cost=0.2)
            if dve_rs:
                dve_rsqrt_col(st[:, 3:4], st[:, 4:5], rcol, st_name, st_name)
            else:
                act(lambda s: s.activation(out=st[:, 4:5], in_=st[:, 3:4],
                                           func=AF.Sqrt),
                    reads=st_name, writes=st_name)
                dve(lambda v: v.reciprocal(out=rcol, in_=st[:, 4:5]),
                    reads=st_name, writes=st_name)
            tok = dve(lambda v: v.tensor_scalar(
                out=nmrcol, in0=mcol, scalar1=rcol, scalar2=-1.0,
                op0=ALU.mult, op1=ALU.mult),
                reads=st_name, writes=st_name, cost=0.2)
            p.chunk()
            return tok

        def quantize(b, dst_sb, dst_name, src_sb, src_name, rstd_ap=None,
                     rstd_name=None):
            rd = (src_name,) if rstd_ap is None else (src_name, rstd_name)
            dve(lambda v, s=src_sb: v.tensor_reduce(
                out=b.qbm[:], in_=s.rearrange("p (a c) -> p a c", c=32),
                axis=AX.X, op=ALU.max, apply_absolute_value=True),
                reads=rd, writes=(N(b, "qbm"),))
            if rstd_ap is not None:
                dve(lambda v: v.tensor_scalar(
                    out=b.qs[:], in0=b.qbm[:], scalar1=rstd_ap,
                    scalar2=1.0 / 7.5, op0=ALU.mult, op1=ALU.mult),
                    reads=(N(b, "qbm"),) + ((rstd_name,) if rstd_name else ()),
                    writes=(N(b, "qs"),), cost=0.2)
            else:
                dve(lambda v: v.tensor_scalar_mul(out=b.qs[:], in0=b.qbm[:],
                                                  scalar1=1.0 / 7.5),
                    reads=(N(b, "qbm"),), writes=(N(b, "qs"),), cost=0.2)
            dve(lambda v: v.tensor_scalar_max(out=b.qs[:], in0=b.qs[:],
                                              scalar1=1e-30),
                reads=(N(b, "qs"),), writes=(N(b, "qs"),), cost=0.2)
            p.chunk()
            dve(lambda v: v.reciprocal(out=b.qf[:], in_=b.qs[:]),
                reads=(N(b, "qs"),), writes=(N(b, "qf"),), cost=0.3)
            if rstd_ap is not None:
                dve(lambda v: v.tensor_scalar(
                    out=b.qf[:], in0=b.qf[:], scalar1=rstd_ap, scalar2=0.0,
                    op0=ALU.mult, op1=ALU.add),
                    reads=(N(b, "qf"),) + ((rstd_name,) if rstd_name else ()),
                    writes=(N(b, "qf"),), cost=0.2)
            p.chunk()
            rb = b.qf[:].unsqueeze(2).broadcast_to([128, 32, 32])
            dve(lambda v, s=src_sb, r=rb: v.tensor_tensor(
                out=b.vq.rearrange("p (a c) -> p a c", c=32),
                in0=s.rearrange("p (a c) -> p a c", c=32), in1=r,
                op=ALU.mult),
                reads=(src_name, N(b, "qf")), writes=(N(b, "junkA"),))
            p.chunk()
            act(lambda s: s.activation(out=b.vq[:], in_=b.vq[:],
                                       func=AF.Copy, scale=8.0, bias=MAGIC),
                reads=(N(b, "junkA"),), writes=(N(b, "junkA"),))
            p.chunk()
            act(lambda s: s.activation(out=b.vq[:], in_=b.vq[:],
                                       func=AF.Copy, scale=0.125,
                                       bias=-MAGIC * 0.125),
                reads=(N(b, "junkA"),), writes=(N(b, "junkA"),))
            p.chunk()
            act(lambda s: s.copy(out=b.q8[:], in_=b.vq[:]),
                reads=(N(b, "junkA"),), writes=(N(b, "q8"),))
            p.chunk()
            sbb = b.qs[:].unsqueeze(2).broadcast_to([128, 32, 32])
            return dve(lambda v, d=dst_sb, r=sbb: v.tensor_tensor(
                out=d.rearrange("p (a c) -> p a c", c=32),
                in0=b.q8.rearrange("p (a c) -> p a c", c=32), in1=r,
                op=ALU.mult),
                reads=(N(b, "q8"), N(b, "qs")), writes=(dst_name,))

        def project(mix_sb, mix_name, w_sb, w_name, evac_list, init=None):
            if init is not None:
                for half, PP in ((0, PPa), (1, PPb)):
                    pn = "PPa" if half == 0 else "PPb"
                    pe(lambda t, h=half, P=PP: t.matmul(
                        P[:], ones1[0:1, :],
                        init[0][0:1, 512 * h:512 * h + 512],
                        start=True, stop=False),
                       reads=("ones1", init[1]), writes=(pn,))
            for kc in range(8):
                for half, PP in ((0, PPa), (1, PPb)):
                    pn = "PPa" if half == 0 else "PPb"
                    pe(lambda t, k=kc, h=half, P=PP, m=mix_sb, w=w_sb:
                       t.matmul(P[:], m[:, 128 * k:128 * k + 128],
                                w[:, C * k + 512 * h:C * k + 512 * h + 512],
                                start=(k == 0 and init is None), stop=(k == 7)),
                       reads=(mix_name, w_name), writes=(pn,))
            toks = []
            for half, PP in ((0, PPa), (1, PPb)):
                pn = "PPa" if half == 0 else "PPb"
                toks.append(evac_list[half](PP, pn))
            p.chunk()
            return toks

        b0, b1 = BB[0], BB[1]
        dma(ident[:], idin, writes=("ident",))
        dma(maa_sb[:], maa4, writes=("maa_sb",))
        dma(relay_d_sb[:], relay_d, writes=("relay_d_sb",))
        dma(relay_m_sb[:], relay_m, writes=("relay_m_sb",))
        pdma(bias_sb[:], bias_bc, writes=("bias_sb",))
        dma(b0.junkA[:], decay_bd, writes=("junkA_s",))
        act(lambda s: s.activation(out=e_bd[:], in_=b0.junkA[:], func=AF.Exp),
            reads=("junkA_s",), writes=("x20",))
        dve(lambda v: v.tensor_scalar_min(out=e_bd[:], in0=e_bd[:],
                                          scalar1=ECLAMP),
            reads=("x20",), writes=("x20",))
        act(lambda s: s.activation(
            out=ws_pair[:].unsqueeze(2),
            in_=e_bd.rearrange("p (b j) -> p b j", j=128)[:, :, 0:1],
            func=AF.Exp, scale=-128.0),
            reads=("x20",), writes=("ws_pair",))
        epv = e_bd.rearrange("p (b j) -> p b j", j=128)[:, :, 0]
        for r in range(4):
            dve(lambda v, rr=r, ep=epv: v.tensor_scalar(
                out=fac4[:, 8 * rr:8 * rr + 8], in0=ep,
                scalar1=relay_d_sb[:, rr:rr + 1], scalar2=0.0,
                op0=ALU.mult, op1=ALU.add),
                reads=("x20", "relay_d_sb"), writes=("fac4",))
            act(lambda s, rr=r: s.activation(
                out=fac4[:, 8 * rr:8 * rr + 8],
                in_=fac4[:, 8 * rr:8 * rr + 8], func=AF.Exp, scale=-2048.0),
                reads=("fac4",), writes=("fac4",))
            dve(lambda v, rr=r: v.tensor_scalar(
                out=fac4[:, 8 * rr:8 * rr + 8],
                in0=fac4[:, 8 * rr:8 * rr + 8],
                scalar1=relay_m_sb[:, rr:rr + 1], scalar2=0.0,
                op0=ALU.mult, op1=ALU.add),
                reads=("fac4", "relay_m_sb"), writes=("fac4",))
        dma(b0.junkB[:], qpat, writes=("junkB_s",))
        dve(lambda v: v.tensor_mul(out=b0.junkB[:], in0=b0.junkB[:],
                                   in1=e_bd[:]),
            reads=("junkB_s", "x20"), writes=("junkB_s",))
        act(lambda s: s.activation(out=wb_all[:], in_=b0.junkB[:],
                                   func=AF.Exp, scale=-1.0),
            reads=("junkB_s",), writes=("wb_all",))
        pdma(lnw_sb[:], lnw_bc, writes=("lnw_sb",))
        pdma(lnb_sb[:], lnb_bc, writes=("lnb_sb",))
        dma(b0.ff_tm[:], decay_hd, writes=("ff_tm_s",))
        dma(b0.stA[:, 0:1], iotarev, writes=("stA0",))
        act(lambda s: s.activation(out=b0.ff_tm[:], in_=b0.ff_tm[:],
                                   func=AF.Exp),
            reads=("ff_tm_s",), writes=("ff_tm_s",))
        dve(lambda v: v.tensor_scalar_min(out=b0.ff_tm[:], in0=b0.ff_tm[:],
                                          scalar1=ECLAMP),
            reads=("ff_tm_s",), writes=("ff_tm_s",))
        act(lambda s: s.activation(out=wk_tm[:], in_=b0.ff_tm[:], func=AF.Exp,
                                   scale=b0.stA[:, 0:1]),
            reads=("ff_tm_s", "stA0"), writes=("wk_tm",))
        for hf in range(2):
            cw = slice(C * hf, C * hf + C)
            dma(b0.junkA[:], decay_wm[:, cw], writes=("junkA_s",))
            act(lambda s: s.activation(out=b0.junkA[:], in_=b0.junkA[:],
                                       func=AF.Exp),
                reads=("junkA_s",), writes=("junkA_s",))
            dve(lambda v: v.tensor_scalar_min(out=b0.junkA[:],
                                              in0=b0.junkA[:],
                                              scalar1=ECLAMP),
                reads=("junkA_s",), writes=("junkA_s",))
            dma(b0.ff_tm[:], imj[:, cw], writes=("ff_tm_s",))
            dve(lambda v: v.tensor_mul(out=b0.junkA[:], in0=b0.junkA[:],
                                       in1=b0.ff_tm[:]),
                reads=("junkA_s", "ff_tm_s"), writes=("junkA_s",))
            act(lambda s: s.activation(out=b0.junkA[:], in_=b0.junkA[:],
                                       func=AF.Exp, scale=-1.0),
                reads=("junkA_s",), writes=("junkA_s",))
            dma(b0.ff_tm[:], maskio[:, cw], writes=("ff_tm_s",))
            dve(lambda v: v.tensor_mul(out=b0.junkA[:], in0=b0.junkA[:],
                                       in1=b0.ff_tm[:]),
                reads=("junkA_s", "ff_tm_s"), writes=("junkA_s",))
            dma(b0.ff_tm[:], udiag[:, cw], writes=("ff_tm_s",))
            dve(lambda v, h=hf: v.tensor_add(out=wm_b[:, C * h:C * h + C],
                                             in0=b0.junkA[:],
                                             in1=b0.ff_tm[:]),
                reads=("junkA_s", "ff_tm_s"), writes=("wm_b",))

        dve(lambda v: v.memset(S[:], 0.0), writes=("S",))
        dve(lambda v: v.memset(ones1[:], 1.0), writes=("ones1",))

        lnh = b0.ln_tm
        stH = b0.stA
        dma(b0.xbuf[0:1, :], xin[0:1, :], writes=("xbuf0",))
        dve(lambda v: v.tensor_scalar(
            out=b0.junkA[0:1, :], in0=b0.xbuf[0:1, :], scalar1=1.0,
            scalar2=0.0, op0=ALU.mult, op1=ALU.add,
            accum_out=stH[0:1, 0:1]),
            reads=("xbuf0",), writes=("junkA_s", "stA0"))
        act(lambda s: s.activation(out=b0.junkB[0:1, :], in_=b0.xbuf[0:1, :],
                                   func=AF.Square, accum_out=stH[0:1, 1:2]),
            reads=("xbuf0",), writes=("junkB_s", "stA0"))
        dve(lambda v: v.tensor_scalar_mul(out=stH[0:1, 2:3],
                                          in0=stH[0:1, 0:1],
                                          scalar1=1.0 / C),
            reads=("stA0",), writes=("stA0",))
        dve(lambda v: v.tensor_scalar(
            out=stH[0:1, 3:4], in0=stH[0:1, 2:3], scalar1=stH[0:1, 2:3],
            scalar2=EPS_LN, op0=ALU.mult, op1=ALU.subtract),
            reads=("stA0",), writes=("stA0",))
        dve(lambda v: v.tensor_scalar(
            out=stH[0:1, 4:5], in0=stH[0:1, 1:2], scalar1=1.0 / C,
            scalar2=stH[0:1, 3:4], op0=ALU.mult, op1=ALU.subtract),
            reads=("stA0",), writes=("stA0",))
        act(lambda s: s.activation(out=stH[0:1, 5:6], in_=stH[0:1, 4:5],
                                   func=AF.Sqrt),
            reads=("stA0",), writes=("stA0",))
        dve(lambda v: v.reciprocal(out=stH[0:1, 6:7], in_=stH[0:1, 5:6]),
            reads=("stA0",), writes=("stA0",))
        dve(lambda v: v.tensor_scalar(
            out=stH[0:1, 7:8], in0=stH[0:1, 2:3], scalar1=stH[0:1, 6:7],
            scalar2=-1.0, op0=ALU.mult, op1=ALU.mult),
            reads=("stA0",), writes=("stA0",))
        act(lambda s: s.activation(out=lnh[0:1, 0:C], in_=b0.xbuf[0:1, :],
                                   func=AF.Identity, scale=stH[0:1, 6:7],
                                   bias=stH[0:1, 7:8]),
            reads=("xbuf0", "stA0"), writes=("ln_tm0",))
        pdma(cc_in.ap()[0:1, :], lnh[0:1, 0:C],
             reads=("ln_tm0",), writes=("cc_in",))
        pdma(halo0[:],
             cc_in.ap()[0:1, :].rearrange("a (cc q) -> (a q) cc", q=128),
             reads=("cc_in",), writes=("halo0",),
             allow_slow_non_contiguous=True)

        def load_weight(dst_sb, dst_name, src):
            for cc in range(8):
                pdma(dst_sb[:, C * cc:C * cc + C],
                     src[128 * cc:128 * cc + 128, :], writes=(dst_name,))

        load_weight(WA, "WA", wdr["wkT"])
        load_weight(WB, "WB", wdr["wvT"])
        load_weight(WC, "WC", wdr["wrT"])
        load_weight(WD, "WD", wdr["wgT"])
        load_weight(WF, "WF", wdr["woT"])

        for ob in range(8):
            dma(b1.xbuf[:], wffn[128 * ob:128 * ob + 128, :],
                writes=("xbuf1",))
            quantize(b1, b1.xd, "xd_s", b1.xbuf, "xbuf1")
            for cc in range(8):
                P, ptn = ((PT, "PT"), (PT2, "PT2"))[cc % 2]
                pe(lambda t, c=cc, P=P: t.transpose(
                    P[:, 128 * (c // 2):128 * (c // 2) + 128],
                    b1.xd[:, 128 * c:128 * c + 128], ident[:]),
                   reads=("xd_s", "ident"), writes=(ptn,))
            wdv = WE.rearrange("p (cc e ob q) -> p cc e ob q", e=2, ob=8,
                               q=128)
            act(lambda s, o=ob, w=wdv: s.copy(
                out=w[:, :, 0, o, :],
                in_=PT.rearrange("p (cc q) -> p cc q", q=128)[:, 0:4, :]),
                reads=("PT",), writes=("WE",))
            act(lambda s, o=ob, w=wdv: s.copy(
                out=w[:, :, 1, o, :],
                in_=PT2.rearrange("p (cc q) -> p cc q", q=128)[:, 0:4, :]),
                reads=("PT2",), writes=("WE",))

        dump("wd", WE[:, 0:C], ("WE",))
        dump("wk_tm", wk_tm[:], ("wk_tm",))
        dump("halo0", halo0[:], ("halo0",))

        def x_load(b, t):
            dma(b.xbuf[:], xin[1 + TILE * t:1 + TILE * t + TILE, :],
                writes=(N(b, "xbuf"),))

        def p1_stats(b, t):
            ln_stats(b, b.xbuf, N(b, "xbuf"), b.stA, N(b, "stA"),
                     m1[:, t:t + 1], rstd1[:, t:t + 1], nmr1[:, t:t + 1],
                     EPS_LN, outname="ls" + str(t % 4))

        def ln_and_mixes(b, t, ca, cb, stats, ntmax, pf=1):
            if pf == 1 and t + 1 < ntmax:
                x_load(BB[(t + 1) % 2], t + 1)
            if stats:
                p1_stats(b, t)
            act(lambda s, tt=t: s.activation(
                out=b.ln_tm[:], in_=b.xbuf[:], func=AF.Identity,
                scale=rstd1[:, tt:tt + 1], bias=nmr1[:, tt:tt + 1]),
                reads=(N(b, "xbuf"), "ls" + str(t % 4)),
                writes=(N(b, "ln_tm"),))
            if pf == 2 and t + 2 < ntmax:
                x_load(BB[t % 2], t + 2)
            transpose_set(b.ln_tm, N(b, "ln_tm"), b.ln_fm, N(b, "ln_fm"))
            lv = b.ln_fm.rearrange("p (cc q) -> p cc q", q=128)
            dv = b.d_fm.rearrange("p (cc q) -> p cc q", q=128)
            d_eng = pool if pf == 1 else dve
            d_eng(lambda g, lv=lv, dv=dv: g.tensor_tensor(
                out=dv[:, :, 1:128], in0=lv[:, :, 0:127],
                in1=lv[:, :, 1:128], op=ALU.subtract),
                reads=(N(b, "ln_fm"),), writes=(N(b, "d_fm"),))
            dve(lambda v, lv=lv, dv=dv: v.tensor_tensor(
                out=dv[:, :, 0:1], in0=halo[:].unsqueeze(2),
                in1=lv[:, :, 0:1], op=ALU.subtract),
                reads=(N(b, "ln_fm"), "halo"), writes=(N(b, "d_fm"),))
            dve(lambda v, lv=lv: v.tensor_copy(out=halo[:],
                                               in_=lv[:, :, 127]),
                reads=(N(b, "ln_fm"),), writes=("halo",))
            mix2(b, ca, cb, b.mixA, N(b, "mixA"), b.mixB, N(b, "mixB"))

        def mix2(b, ca, cb, mA, mAn, mB, mBn):
            dv3 = b.d_fm.rearrange("p (c q) -> p c q", q=128)
            maa_a = maa_sb[:, ca:ca + 8].unsqueeze(2).broadcast_to(
                [128, 8, 128])
            maa_b = maa_sb[:, cb:cb + 8].unsqueeze(2).broadcast_to(
                [128, 8, 128])
            dve(lambda v, o=mA.rearrange("p (c q) -> p c q", q=128),
                m=maa_a, i=dv3: v.tensor_tensor(out=o, in0=i, in1=m,
                                                op=ALU.mult),
                reads=(N(b, "d_fm"), "maa_sb"), writes=(mAn,))
            dve(lambda v, o=mA: v.tensor_add(out=o[:], in0=o[:],
                                             in1=b.ln_fm[:]),
                reads=(mAn, N(b, "ln_fm")), writes=(mAn,))
            p.chunk()
            dve(lambda v, o=mB.rearrange("p (c q) -> p c q", q=128),
                m=maa_b, i=dv3: v.tensor_tensor(out=o, in0=i, in1=m,
                                                op=ALU.mult),
                reads=(N(b, "d_fm"), "maa_sb"), writes=(mBn,))
            dve(lambda v, o=mB: v.tensor_add(out=o[:], in0=o[:],
                                             in1=b.ln_fm[:]),
                reads=(mBn, N(b, "ln_fm")), writes=(mBn,))
            p.chunk()

        def proj_to(b, mix, mix_name, w, w_name, dst, dst_name, gelu=False,
                    accum=None):
            def fn(s, P, lo, ac):
                if gelu:
                    return s.activation(out=dst[:, lo:lo + 512], in_=P[:],
                                        func=AF.Gelu)
                return s.activation(out=dst[:, lo:lo + 512], in_=P[:],
                                    func=AF.Copy,
                                    accum_out=(ac if accum else None))
            wr = (dst_name,) if accum is None else (dst_name, accum[1])
            project(mix, mix_name, w, w_name, [
                (lambda PP, pn, lo=0: act(
                    lambda s, P=PP, l=lo: fn(
                        s, P, l, accum[0][0] if accum else None),
                    reads=(pn,), writes=wr)),
                (lambda PP, pn, lo=512: act(
                    lambda s, P=PP, l=lo: fn(
                        s, P, l, accum[0][1] if accum else None),
                    reads=(pn,), writes=wr)),
            ])

        def kv_block(b):
            proj_to(b, b.mixA, N(b, "mixA"), WKCUR[0], WKCUR[1],
                    b.k_tm, N(b, "k_tm"))
            proj_to(b, b.mixB, N(b, "mixB"), WVCUR[0], WVCUR[1],
                    b.v_tm, N(b, "v_tm"))

        def attn_A_and_scan(b, use_pool=True):
            kwk_eng = pool if use_pool else dve
            kwk_eng(lambda g: g.tensor_mul(out=b.kwk[:], in0=b.k_tm[:],
                                           in1=wk_tm[:]),
                    reads=(N(b, "k_tm"), "wk_tm"), writes=(N(b, "kwk"),))
            for pb in range(8):
                PAx, pn = ((PAe, "PAe"), (PAo, "PAo"))[pb // 4]
                pe(lambda tt, bb=pb, P=PAx: tt.matmul(
                    P[:, 128 * (bb % 4):128 * (bb % 4) + 128],
                    b.kwk[:, 128 * bb:128 * bb + 128],
                    b.v_tm[:, 128 * bb:128 * bb + 128],
                    start=True, stop=True),
                   reads=(N(b, "kwk"), N(b, "v_tm")), writes=(pn,))
            p.chunk()
            S3 = S.rearrange("p (b q) -> p b q", q=128)
            wsb = ws_pair[:].unsqueeze(2).broadcast_to([128, 8, 128])
            dve(lambda v, o=S3, m=wsb: v.tensor_tensor(
                out=o, in0=o, in1=m, op=ALU.mult),
                reads=("S", "ws_pair"), writes=("S",))
            p.chunk()
            Se = S.rearrange("p (b two n) -> p b two n", two=2, n=64)
            for half, PAx, pn in ((0, PAe, "PAe"), (1, PAo, "PAo")):
                PAv = PAx.rearrange("p (b two n) -> p b two n", two=2, n=64)
                dve(lambda v, Se=Se, PAv=PAv, h4=4 * half:
                    v.scalar_tensor_tensor(
                        out=Se[0:64, h4:h4 + 4, 0, :],
                        in0=Se[0:64, h4:h4 + 4, 0, :], scalar=1.0,
                        in1=PAv[0:64, :, 0, :], op0=ALU.mult, op1=ALU.add),
                    reads=("S", pn), writes=("S",))
                dve(lambda v, Se=Se, PAv=PAv, h4=4 * half:
                    v.scalar_tensor_tensor(
                        out=Se[64:128, h4:h4 + 4, 1, :],
                        in0=Se[64:128, h4:h4 + 4, 1, :], scalar=1.0,
                        in1=PAv[64:128, :, 1, :], op0=ALU.mult, op1=ALU.add),
                    reads=("S", pn), writes=("S",))

        WKCUR = (WA, "WA")
        WVCUR = (WB, "WB")
        dve(lambda v: v.tensor_copy(out=halo[:], in_=halo0[:]),
            reads=("halo0",), writes=("halo",))
        x_load(BB[0], 0)
        x_load(BB[1], 1)

        for t in range(n_tiles + 3):
            stages = []
            if t < n_tiles:
                stages.append(lambda tt=t: p1_stats(BB[tt % 2], tt))
            if 0 <= t - 1 < n_tiles:
                stages.append(lambda tt=t: ln_and_mixes(
                    BB[(tt - 1) % 2], tt - 1, 0, 8, stats=False,
                    ntmax=n_tiles, pf=2))
            if 0 <= t - 2 < n_tiles:
                stages.append(lambda tt=t: kv_block(BB[(tt - 2) % 2]))
            if t >= 3:
                stages.append(lambda tt=t: attn_A_and_scan(BB[(tt - 3) % 2],
                                                           use_pool=False))
            p.staged(*stages)

        dump("p1M", S[:], ("S",))

        if not phase2:
            p.barrier_wait("sync", [(s, p.count.get(s, 0)) for s in SOUTS])
            emit(nc, p)
            return nc

        def cc_trigger():
            pdma(cc_in.ap(), S[:], reads=("S",), writes=("cc_in",))
            p.op("gpsimd",
                 lambda g: g.collective_compute(
                     "AllGather", ALU.bypass,
                     replica_groups=[[0, 1, 2, 3], [4, 5, 6, 7]],
                     ins=[cc_in.ap().opt()], outs=[cc_out.ap().opt()]),
                 sCC, 1, reads=("cc_in",), writes=("cc_out",))
            p.chunk()

        def combine():
            S3 = S.rearrange("p (b q) -> p b q", q=128)
            for r in range(4):
                mb, mn = BB[r % 2].t0, N(BB[r % 2], "t0")
                pdma(mb[:], cc_out.ap()[128 * r:128 * r + 128, :],
                     reads=("cc_out",), writes=(mn,))
                fb = fac4[:, 8 * r:8 * r + 8].unsqueeze(2).broadcast_to(
                    [128, 8, 128])
                m3 = mb.rearrange("p (b q) -> p b q", q=128)
                if r == 0:
                    dve(lambda v, m=m3, f=fb: v.tensor_tensor(
                        out=S3, in0=m, in1=f, op=ALU.mult),
                        reads=(mn, "fac4"), writes=("S",))
                else:
                    dve(lambda v, m=m3, f=fb: v.tensor_tensor(
                        out=b0.junkB.rearrange("p (b q) -> p b q", q=128),
                        in0=m, in1=f, op=ALU.mult),
                        reads=(mn, "fac4"), writes=("junkB_s",))
                    dve(lambda v: v.tensor_add(out=S[:], in0=S[:],
                                               in1=b0.junkB[:]),
                        reads=("S", "junkB_s"), writes=("S",))
            dump("Sin", S[:], ("S",))



        dve(lambda v: v.tensor_copy(out=halo[:], in_=halo0[:]),
            reads=("halo0",), writes=("halo",))

        def p2_front(b, t):
            ln_and_mixes(b, t, 0, 8, stats=False, ntmax=NP2)
            kv_block(b)
            transpose_set(b.k_tm, N(b, "k_tm"), b.k_fm, N(b, "k_fm"))
            mix2(b, 16, 24, b.mixA, N(b, "mixA"), b.mixB, N(b, "mixB"))
            proj_to(b, b.mixA, N(b, "mixA"), WC, "WC", b.r_tm, N(b, "ln_tm"))
            proj_to(b, b.mixB, N(b, "mixB"), WD, "WD", b.g_tm, N(b, "g_tm"),
                    gelu=True)
            transpose_set(b.r_tm, N(b, "ln_tm"), b.r_fm, N(b, "r_fm"))
            pool(lambda g: g.tensor_mul(out=b.rwb[:], in0=b.r_fm[:],
                                        in1=wb_all[:]),
                 reads=(N(b, "r_fm"), "wb_all"), writes=(N(b, "rwb"),))

        def p2_mid(b, t):
            pool(lambda g: g.tensor_copy(out=b.S_bf[:], in_=S[:]),
                 reads=("S",), writes=(N(b, "d_fm"),))
            attv = b.attW.rearrange("p (b par q) -> p b par q", par=2, q=128)
            wmv = wm_b.rearrange("p (g b par q) -> p g b par q", g=2, par=2,
                                 q=128)
            for grp, PYx, pyn in ((0, PY, "PY"), (1, PY2, "PY2")):
                for i in range(4):
                    pb = 4 * grp + i
                    for par, PAx, pname in ((0, PAe, "PAe"), (1, PAo, "PAo")):
                        pe(lambda tt, bb=pb, pp=par, ii=i, P=PAx: tt.matmul(
                            P[:, 128 * ii:128 * ii + 128],
                            b.k_fm[64 * pp:64 * pp + 64,
                                   128 * bb:128 * bb + 128],
                            b.r_fm[64 * pp:64 * pp + 64,
                                   128 * bb:128 * bb + 128],
                            start=True, stop=True),
                           reads=(N(b, "k_fm"), N(b, "r_fm")),
                           writes=(pname,))
                for par, PAx, pname in ((0, PAe, "PAe"), (1, PAo, "PAo")):
                    dve(lambda v, g2=grp, pp=par, P=PAx, av=attv, wv=wmv:
                        v.tensor_tensor(
                            out=av[:, :, pp, :],
                            in0=P.rearrange("p (i q) -> p i q", q=128),
                            in1=wv[:, g2, :, pp, :], op=ALU.mult),
                        reads=(pname, "wm_b"), writes=("attW" + str(par),))
                for i in range(8):
                    h = 8 * grp + i
                    pe(lambda tt, hh=h, ii=i, P=PYx: tt.matmul(
                        P[:, 64 * ii:64 * ii + 64],
                        b.attW[:, 128 * ii:128 * ii + 128],
                        b.v_tm[:, 64 * hh:64 * hh + 64],
                        start=(ii == 0), stop=False),
                       reads=("attW" + str(i % 2), N(b, "v_tm")),
                       writes=(pyn,))
                for j in range(4):
                    pb = 4 * grp + j
                    pe(lambda tt, bb=pb, jj=j, P=PYx: tt.matmul(
                        P[:, 128 * jj:128 * jj + 128],
                        b.rwb[:, 128 * bb:128 * bb + 128],
                        b.S_bf[:, 128 * bb:128 * bb + 128],
                        start=False, stop=(jj == 3)),
                       reads=(N(b, "rwb"), N(b, "d_fm")), writes=(pyn,))
                act(lambda s, g2=grp, P=PYx: s.copy(
                    out=b.y_bf[:, 512 * g2:512 * g2 + 512], in_=P[:]),
                    reads=(pyn,), writes=(N(b, "y_bf"),))
                p.chunk()
            attn_A_and_scan(b)
            p.chunk()
            act(lambda s: s.activation(out=b.ysq[:], in_=b.y_bf[:],
                                       func=AF.Square),
                reads=(N(b, "y_bf"),), writes=("attW0", "attW1"))
            dve(lambda v: v.tensor_reduce(
                out=b.gsum[:], in_=b.y_bf.rearrange("p (h n) -> p h n", n=64),
                axis=AX.X, op=ALU.add),
                reads=(N(b, "y_bf"),), writes=(N(b, "gsum"),))
            dve(lambda v: v.tensor_reduce(
                out=b.gsq[:], in_=b.ysq.rearrange("p (h n) -> p h n", n=64),
                axis=AX.X, op=ALU.add),
                reads=("attW0", "attW1"), writes=(N(b, "gsq"),))
            p.chunk()
            dve(lambda v: v.tensor_scalar_mul(out=b.gn_m[:], in0=b.gsum[:],
                                              scalar1=1.0 / 64),
                reads=(N(b, "gsum"),), writes=(N(b, "gn_m"),), cost=0.2)
            act(lambda s: s.activation(out=b.gs1[:], in_=b.gn_m[:],
                                       func=AF.Square),
                reads=(N(b, "gn_m"),), writes=(N(b, "gs1"),), cost=0.3)
            dve(lambda v: v.scalar_tensor_tensor(
                out=b.gs1[:], in0=b.gsq[:], scalar=1.0 / 64, in1=b.gs1[:],
                op0=ALU.mult, op1=ALU.subtract),
                reads=(N(b, "gsq"), N(b, "gs1")), writes=(N(b, "gs1"),),
                cost=0.2)
            dve(lambda v: v.tensor_scalar_add(out=b.gs1[:], in0=b.gs1[:],
                                              scalar1=EPS_GN),
                reads=(N(b, "gs1"),), writes=(N(b, "gs1"),), cost=0.2)
            p.chunk()
            dve_rsqrt_col(b.gs1[:], b.gsum[:], b.gn_r[:],
                          (N(b, "gs1"), N(b, "gsum")),
                          (N(b, "gsum"), N(b, "gn_r")))
            p.chunk()
            pool(lambda g: g.tensor_mul(out=b.G1[:], in0=b.g_tm[:],
                                        in1=lnw_sb[:]),
                 reads=(N(b, "g_tm"), "lnw_sb"), writes=(N(b, "mixA"),))
            pool(lambda g: g.tensor_mul(out=b.G2[:], in0=b.g_tm[:],
                                        in1=lnb_sb[:]),
                 reads=(N(b, "g_tm"), "lnb_sb"), writes=(N(b, "mixB"),))
            gm_b = b.gn_m[:].unsqueeze(2).broadcast_to([128, H, 64])
            gr_b = b.gn_r[:].unsqueeze(2).broadcast_to([128, H, 64])
            y3 = b.y_bf.rearrange("p (h n) -> p h n", n=64)
            q3 = b.ysq.rearrange("p (h n) -> p h n", n=64)
            z3 = b.z1.rearrange("p (h n) -> p h n", n=64)
            dve(lambda v, y3=y3, q3=q3, m=gm_b: v.tensor_tensor(
                out=q3, in0=y3, in1=m, op=ALU.subtract),
                reads=(N(b, "y_bf"), N(b, "gn_m")), writes=("attW0", "attW1"))
            dve(lambda v, q3=q3, z3=z3, r=gr_b: v.tensor_tensor(
                out=z3, in0=q3, in1=r, op=ALU.mult),
                reads=("attW0", "attW1", N(b, "gn_r")),
                writes=(N(b, "z1"),))
            dve(lambda v: v.tensor_mul(out=b.ysq[:], in0=b.z1[:],
                                       in1=b.G1[:]),
                reads=(N(b, "z1"), N(b, "mixA")), writes=("attW0", "attW1"))
            dve(lambda v: v.tensor_add(out=b.z1[:], in0=b.ysq[:],
                                       in1=b.G2[:]),
                reads=("attW0", "attW1", N(b, "mixB")),
                writes=(N(b, "z1"),))
            p.chunk()

        def p2_tail_a(b, t):
            dma(b.junkB[:], xin[1 + TILE * t:1 + TILE * t + TILE, :],
                writes=(N(b, "junkB"),))
            p.chunk()
            transpose_set(b.z1, N(b, "z1"), b.z_fm, N(b, "z_fm"))
            proj_to(b, b.z_fm, N(b, "z_fm"), WF, "WF", b.ao, N(b, "t0"),
                    accum=((b.stW[:, 4:5], b.stW[:, 2:3]), N(b, "stW")))
            ln_stats(b, b.ao, N(b, "t0"), b.stW, N(b, "stW"), b.stW[:, 5:6],
                     b.stW[:, 6:7], b.stW[:, 7:8], EPS_LN, dve_rs=True,
                     jB=(b.x2, N(b, "x2")),
                     pre_sum=(b.stW[:, 4:5], b.stW[:, 2:3]))
            act(lambda s: s.activation(out=b.x2[:], in_=b.ao[:],
                                       func=AF.Identity,
                                       scale=b.stW[:, 6:7],
                                       bias=b.stW[:, 7:8]),
                reads=(N(b, "t0"), N(b, "stW")), writes=(N(b, "x2"),))
            dve(lambda v: v.tensor_add(out=b.x2[:], in0=b.junkB[:],
                                       in1=b.x2[:]),
                reads=(N(b, "junkB"), N(b, "x2")), writes=(N(b, "x2"),))
            p.chunk()

        def p2_tail_b(b, t):
            ln_stats(b, b.x2, N(b, "x2"), b.stX, N(b, "stX"), b.stX[:, 5:6],
                     b.stX[:, 6:7], b.stX[:, 7:8], EPS_LN, dve_rs=True,
                     jB=(b.junkA, N(b, "junkA")))
            dve(lambda v: v.tensor_scalar_mul(out=b.stX[:, 5:6],
                                              in0=b.stX[:, 5:6],
                                              scalar1=-1.0),
                reads=(N(b, "stX"),), writes=(N(b, "stX"),), cost=0.2)
            act(lambda s: s.activation(out=b.ff_tm[:], in_=b.x2[:],
                                       func=AF.Identity,
                                       bias=b.stX[:, 5:6]),
                reads=(N(b, "x2"), N(b, "stX")), writes=(N(b, "ff_tm"),))
            quantize(b, b.xd, N(b, "xd"), b.ff_tm, N(b, "ff_tm"),
                     rstd_ap=b.stX[:, 6:7], rstd_name=N(b, "stX"))
            p.chunk()
            transpose_set(b.xd, N(b, "xd"), b.xd_fm, N(b, "xd_fm"))
            project(b.xd_fm, N(b, "xd_fm"), WE, "WE", [
                (lambda PP, pn: act(
                    lambda s, P=PP: s.activation(
                        out=b.ff_tm[:, 0:512], in_=P[:], func=AF.Copy,
                        accum_out=b.stF[:, 4:5]),
                    reads=(pn,), writes=(N(b, "ff_tm"), N(b, "stF")))),
                (lambda PP, pn: act(
                    lambda s, P=PP: s.activation(
                        out=b.ff_tm[:, 512:1024], in_=P[:], func=AF.Copy,
                        accum_out=b.stF[:, 2:3]),
                    reads=(pn,), writes=(N(b, "ff_tm"), N(b, "stF")))),
            ], init=(bias_sb, "bias_sb"))
            ln_stats(b, b.ff_tm, N(b, "ff_tm"), b.stF, N(b, "stF"),
                     b.stF[:, 5:6], b.stF[:, 6:7], b.stF[:, 7:8], EPS_LN,
                     dve_rs=True, jB=(b.junkA, N(b, "junkA")),
                     pre_sum=(b.stF[:, 4:5], b.stF[:, 2:3]))
            act(lambda s: s.activation(out=b.junkA[:], in_=b.ff_tm[:],
                                       func=AF.Identity,
                                       scale=b.stF[:, 6:7],
                                       bias=b.stF[:, 7:8]),
                reads=(N(b, "ff_tm"), N(b, "stF")), writes=(N(b, "junkA"),))
            pool(lambda g: g.tensor_add(out=b.ff_tm[:], in0=b.x2[:],
                                        in1=b.junkA[:]),
                 reads=(N(b, "x2"), N(b, "junkA")), writes=(N(b, "ff_tm"),))
            dma_out(yout[TILE * t:TILE * t + TILE, :], b.ff_tm[:],
                    reads=(N(b, "ff_tm"),))

        NP2 = p2_tiles if p2_tiles is not None else n_tiles
        x_load(BB[0], 0)
        p.staged(cc_trigger, lambda: p2_front(BB[0], 0))
        if NP2 > 1:
            p.staged(lambda: p2_front(BB[1], 1))
        combine()
        p.staged(lambda: p2_mid(BB[0], 0))
        for s in range(2, NP2 + 3):
            stages = []
            if s < NP2:
                stages.append(lambda tt=s: p2_front(BB[tt % 2], tt))
            if s - 1 < NP2:
                stages.append(lambda tt=s: p2_mid(BB[(tt - 1) % 2], tt - 1))
            if 0 <= s - 2 < NP2:
                stages.append(
                    lambda tt=s: p2_tail_a(BB[(tt - 2) % 2], tt - 2))
            if 0 <= s - 3 < NP2:
                stages.append(
                    lambda tt=s: p2_tail_b(BB[(tt - 3) % 2], tt - 3))
            p.staged(*stages)

        p.barrier_wait("sync", [(s, p.count.get(s, 0)) for s in SOUTS])
        emit(nc, p)
    return nc


LAST_PROG = None


def emit(nc, p):
    global LAST_PROG
    LAST_PROG = p
    with nc.Block() as blk:
        blk.sync(p.runner("sync"))
        blk.tensor(p.runner("tensor"))
        blk.vector(p.runner("vector"))
        blk.scalar(p.runner("scalar"))
        blk.gpsimd(p.runner("gpsimd"))



def make_host_consts():
    c = {}
    q = np.arange(128, dtype=np.float32)
    c["qpat"] = np.tile(q, 8)[None, :].repeat(128, 0).astype(np.float32)
    j = np.arange(128, dtype=np.float32)[:, None]
    i = np.arange(128, dtype=np.float32)[None, :]
    c["imj"] = np.tile(np.maximum(i - j - 1.0, 0.0), (1, 16)).astype(
        np.float32)
    c["maskio"] = np.tile((i > j).astype(np.float32), (1, 16))
    c["iotarev"] = -(127.0 - np.arange(128, dtype=np.float32))[:, None].astype(
        np.float32)
    c["idin"] = np.eye(128).astype(ml_dtypes.bfloat16)
    return c


def make_in_maps(inp):
    return _make_in_maps(**inp)


def _make_in_maps(L0_attn_x, L0_attn_time_maa_k, L0_attn_time_maa_v,
                  L0_attn_time_maa_r, L0_attn_time_maa_g, L0_attn_time_decay,
                  L0_attn_time_faaaa, L0_attn_Wr, L0_attn_Wk, L0_attn_Wv,
                  L0_attn_Wg, L0_attn_Wo, L0_attn_ln_w, L0_attn_ln_b,
                  L0_ffn_weight, L0_ffn_bias):
    x = np.ascontiguousarray(np.asarray(L0_attn_x, dtype=np.float32))
    decay = np.asarray(L0_attn_time_decay, dtype=np.float32).reshape(H)
    faaaa = np.asarray(L0_attn_time_faaaa, dtype=np.float32).reshape(H)

    common = make_host_consts()
    common["wkT"] = np.ascontiguousarray(np.asarray(L0_attn_Wk).T)
    common["wvT"] = np.ascontiguousarray(np.asarray(L0_attn_Wv).T)
    common["wrT"] = np.ascontiguousarray(np.asarray(L0_attn_Wr).T)
    common["wgT"] = np.ascontiguousarray(np.asarray(L0_attn_Wg).T)
    common["woT"] = np.ascontiguousarray(np.asarray(L0_attn_Wo).T)
    common["wffn"] = np.ascontiguousarray(np.asarray(L0_ffn_weight))
    maa = np.zeros((128, 32), np.float32)
    for idx, m in enumerate((L0_attn_time_maa_k, L0_attn_time_maa_v,
                             L0_attn_time_maa_r, L0_attn_time_maa_g)):
        maa[:, 8 * idx:8 * idx + 8] = np.asarray(m).reshape(8, 128).T
    common["maa4"] = maa
    dbd = np.zeros((128, C), np.float32)
    for pb in range(8):
        dbd[0:64, 128 * pb:128 * pb + 128] = decay[2 * pb]
        dbd[64:128, 128 * pb:128 * pb + 128] = decay[2 * pb + 1]
    common["decay_bd"] = dbd
    common["decay_hd"] = np.repeat(decay, 64)[None, :].repeat(128, 0).astype(
        np.float32)
    common["decay_wm"] = np.repeat(decay, 128)[None, :].repeat(128, 0).astype(
        np.float32)
    ud = np.zeros((128, 2 * C), np.float32)
    for h in range(H):
        ud[:, 128 * h:128 * h + 128][np.arange(128), np.arange(128)] = \
            faaaa[h]
    common["udiag"] = ud
    common["lnw_bc"] = np.asarray(L0_attn_ln_w, np.float32)[None, :].repeat(
        128, 0)
    common["lnb_bc"] = np.asarray(L0_attn_ln_b, np.float32)[None, :].repeat(
        128, 0)
    common["bias_bc"] = np.asarray(L0_ffn_bias, np.float32)[None, :].repeat(
        128, 0)

    in_maps = []
    for core in range(8):
        bb, qp = core // 4, core % 4
        im = dict(common)
        xs = np.zeros((TOK + 1, C), np.float32)
        lo = qp * TOK
        xs[1:] = x[bb, lo:lo + TOK]
        if qp > 0:
            xs[0] = x[bb, lo - 1]
        im["x"] = xs
        rd = np.zeros((128, 4), np.float32)
        rm = np.zeros((128, 4), np.float32)
        for r in range(4):
            if r < qp:
                rd[:, r] = float(qp - 1 - r)
                rm[:, r] = 1.0
            else:
                rd[:, r] = 1e30
        im["relay_d"] = rd
        im["relay_m"] = rm
        in_maps.append(im)
    return in_maps


def kernel(**inputs):
    in_maps = make_in_maps(inputs)
    x = np.asarray(inputs["L0_attn_x"])
    B, T, _ = x.shape
    nc = build_nc()
    res = run_bass_kernel_spmd(nc, in_maps, core_ids=list(range(8)))
    out = np.zeros((B, T, C), np.float32)
    for core in range(8):
        bb, qp = core // 4, core % 4
        out[bb, qp * TOK:(qp + 1) * TOK] = res.results[core]["y"]
    return out


if __name__ == "__main__":
    nc = build_nc()
    print("built", nc.n_instructions(), "instructions")


# version 24
# speedup vs baseline: 1.0662x; 1.0662x over previous
import numpy as np
import ml_dtypes
from contextlib import ExitStack

import concourse.bass as bass
import concourse.mybir as mybir
import concourse.bass_utils as _bass_utils
from concourse.bass_utils import run_bass_kernel_spmd

if not getattr(_bass_utils, "_ldw_opt_patched", False):
    _orig_run_command = _bass_utils.run_command

    def _run_command_ldw(cmd, *a, **k):
        cmd = ["--enable-ldw-opt=true" if c == "--enable-ldw-opt=false" else c
               for c in cmd]
        return _orig_run_command(cmd, *a, **k)

    _bass_utils.run_command = _run_command_ldw
    _bass_utils._ldw_opt_patched = True

F32 = mybir.dt.float32
I32 = mybir.dt.int32
BF16 = mybir.dt.bfloat16
FP8 = mybir.dt.float8e4
AF = mybir.ActivationFunctionType
ALU = mybir.AluOpType
AX = mybir.AxisListType

C = 1024
H = 16
TILE = 128
NT = 16
TOK = TILE * NT
EPS_LN = 1e-5
EPS_GN = 1e-5 * 64
MAGIC = float(np.float32(1.5 * 2 ** 23))
ECLAMP = 1e30

ENGS = ("sync", "tensor", "vector", "scalar", "gpsimd")


class Prog:

    def __init__(self):
        self.stream = {e: [] for e in ENGS}
        self.count = {}
        self.bufw = {}
        self.bufr = {}
        self.hwm = {}
        self.defer = None
        self.dma_pools = {}

    def chunk(self):
        if self.defer is not None:
            self.defer.append(None)

    def staged(self, *fns):
        lists = []
        for f in fns:
            self.defer = []
            f()
            chunks, cur = [], []
            for it in self.defer:
                if it is None:
                    if cur:
                        chunks.append(cur)
                    cur = []
                else:
                    cur.append(it)
            if cur:
                chunks.append(cur)
            lists.append(chunks)
        self.defer = None
        ccost = [[sum(it[8] for it in c) for c in l] for l in lists]
        nops = [sum(cl) for cl in ccost]
        done = [0.0] * len(lists)
        idx = [0] * len(lists)
        while True:
            best, bf = -1, 2.0
            for i, l in enumerate(lists):
                if idx[i] < len(l):
                    frac = done[i] / max(nops[i], 1e-9)
                    if frac < bf:
                        best, bf = i, frac
            if best < 0:
                break
            for it in lists[best][idx[best]]:
                self._emit(*it[:8])
            done[best] += ccost[best][idx[best]]
            idx[best] += 1

    DEF_COST = {"vector": 0.7, "scalar": 0.9, "tensor": 0.38, "gpsimd": 2.0,
                "sync": 0.3}

    def op(self, eng, fn, sem, inc_n=1, reads=(), writes=(), waits=(),
           dma_kind=None, cost=None):
        if cost is None:
            cost = self.DEF_COST[eng]
        if self.defer is not None:
            self.defer.append((eng, fn, sem, inc_n, reads, writes, waits,
                               dma_kind, cost))
            return None
        return self._emit(eng, fn, sem, inc_n, reads, writes, waits, dma_kind)

    def _emit(self, eng, fn, sem, inc_n, reads, writes, waits, dma_kind):
        if dma_kind is not None:
            pool, cnt = self.dma_pools[dma_kind]
            sem = pool[cnt[0] % len(pool)]
            cnt[0] += 1
            inc_n = 16
            waits = tuple(waits) + ((sem, self.count.get(sem, 0)),)
        hwm = self.hwm.setdefault(eng, {})
        wl = {}
        for t in waits:
            if t is not None:
                wl[t[0]] = max(wl.get(t[0], 0), t[1])
        for b in reads:
            t = self.bufw.get(b)
            if t is not None:
                wl[t[0]] = max(wl.get(t[0], 0), t[1])
        for b in writes:
            for t in self.bufr.get(b, ()):
                wl[t[0]] = max(wl.get(t[0], 0), t[1])
            t = self.bufw.get(b)
            if t is not None:
                wl[t[0]] = max(wl.get(t[0], 0), t[1])
        self.count[sem] = self.count.get(sem, 0) + inc_n
        tok = (sem, self.count[sem])
        wl2 = []
        for s, v in wl.items():
            if hwm.get(s, -1) >= v:
                continue
            hwm[s] = v
            wl2.append((s, v))
        self.stream[eng].append((fn, tuple(wl2), sem, inc_n))
        for b in reads:
            self.bufr.setdefault(b, []).append(tok)
        for b in writes:
            self.bufw[b] = tok
            self.bufr[b] = []
        return tok

    def barrier_wait(self, eng, waits):
        self.stream[eng].append((None, tuple(waits), None, 0))

    def runner(self, eng_name):
        stream = self.stream[eng_name]

        def run(eng):
            for fn, waits, sem, inc_n in stream:
                for s, val in waits:
                    eng.wait_ge(s, val)
                if fn is None:
                    continue
                inst = fn(eng)
                if sem is not None:
                    inst.then_inc(sem, inc_n)
        return run


def build_nc(debug_names=(), n_tiles=NT, phase2=True, p2_tiles=None):
    nc = bass.Bass("TRN2", target_bir_lowering=False)

    xin = nc.dram_tensor("x", [TOK + 1, C], F32, kind="ExternalInput").ap()
    wdr = {}
    for nm in ("wkT", "wvT", "wrT", "wgT", "woT"):
        wdr[nm] = nc.dram_tensor(nm, [C, C], F32, kind="ExternalInput").ap()
    wffn = nc.dram_tensor("wffn", [C, C], F32, kind="ExternalInput").ap()
    maa4 = nc.dram_tensor("maa4", [128, 32], F32, kind="ExternalInput").ap()
    decay_bd = nc.dram_tensor("decay_bd", [128, C], F32, kind="ExternalInput").ap()
    decay_hd = nc.dram_tensor("decay_hd", [128, C], F32, kind="ExternalInput").ap()
    qpat = nc.dram_tensor("qpat", [128, C], F32, kind="ExternalInput").ap()
    imj = nc.dram_tensor("imj", [128, 2 * C], F32, kind="ExternalInput").ap()
    maskio = nc.dram_tensor("maskio", [128, 2 * C], F32, kind="ExternalInput").ap()
    udiag = nc.dram_tensor("udiag", [128, 2 * C], F32, kind="ExternalInput").ap()
    decay_wm = nc.dram_tensor("decay_wm", [128, 2 * C], F32, kind="ExternalInput").ap()
    iotarev = nc.dram_tensor("iotarev", [128, 1], F32, kind="ExternalInput").ap()
    lnw_bc = nc.dram_tensor("lnw_bc", [128, C], F32, kind="ExternalInput").ap()
    lnb_bc = nc.dram_tensor("lnb_bc", [128, C], F32, kind="ExternalInput").ap()
    bias_bc = nc.dram_tensor("bias_bc", [128, C], F32, kind="ExternalInput").ap()
    idin = nc.dram_tensor("idin", [128, 128], BF16, kind="ExternalInput").ap()
    relay_d = nc.dram_tensor("relay_d", [128, 4], F32, kind="ExternalInput").ap()
    relay_m = nc.dram_tensor("relay_m", [128, 4], F32, kind="ExternalInput").ap()
    yout = nc.dram_tensor("y", [TOK, C], F32, kind="ExternalOutput").ap()
    cc_in = nc.dram_tensor("cc_in", [128, C], F32)
    cc_out = nc.dram_tensor("cc_out", [512, C], F32)
    dbg_dram = {}
    for nm, shp in debug_names:
        dbg_dram[nm] = nc.dram_tensor("dbg_" + nm, list(shp), F32,
                                      kind="ExternalOutput").ap()

    p = Prog()

    with ExitStack() as stck:
        en = stck.enter_context

        def sb(name, shape, dtype):
            return en(nc.sbuf_tensor(name, shape, dtype))

        WA = sb("WA", [128, 8 * C], BF16)
        WB = sb("WB", [128, 8 * C], BF16)
        WC = sb("WC", [128, 8 * C], BF16)
        WD = sb("WD", [128, 8 * C], BF16)
        WE = sb("WE", [128, 8 * C], BF16)
        WF = sb("WF", [128, 8 * C], BF16)
        S = sb("S", [128, C], F32)

        maa_sb = sb("maa_sb", [128, 32], F32)
        wk_tm = sb("wk_tm", [128, C], BF16)
        wb_all = sb("wb_all", [128, C], BF16)
        ws_pair = sb("ws_pair", [128, 8], F32)
        wm_b = sb("wm_b", [128, 2 * C], BF16)
        lnw_sb = sb("lnw_sb", [128, C], BF16)
        lnb_sb = sb("lnb_sb", [128, C], BF16)
        bias_sb = sb("bias_sb", [128, C], BF16)
        ident = sb("ident", [128, 128], BF16)
        ones1 = sb("ones1", [1, 128], BF16)
        relay_d_sb = sb("relay_d_sb", [128, 4], F32)
        relay_m_sb = sb("relay_m_sb", [128, 4], F32)
        fac4 = sb("fac4", [128, 32], F32)
        halo = sb("halo", [128, 8], BF16)
        halo0 = sb("halo0", [128, 8], BF16)
        m1 = sb("m1", [128, NT], F32)
        rstd1 = sb("rstd1", [128, NT], F32)
        nmr1 = sb("nmr1", [128, NT], F32)

        class Bufs:
            pass

        BB = []
        for par in range(2):
            b = Bufs()
            b.p = str(par)
            for nm in ("ln_tm", "d_fm", "mixA", "mixB", "k_tm", "v_tm",
                       "k_fm", "r_fm", "rwb", "g_tm", "z1"):
                setattr(b, nm, sb(nm + b.p, [128, C], BF16))
            for nm in ("xbuf", "x2"):
                setattr(b, nm, sb(nm + b.p, [128, C], F32))
            for nm in ("stA", "stW", "stX", "stF"):
                setattr(b, nm, sb(nm + b.p, [128, 8], F32))
            BB.append(b)
        shared_gn = {nm: sb(nm + "_s", [128, H], F32)
                     for nm in ("gn_m", "gn_r", "gsum", "gsq", "gs1")}
        shared_q = {nm: sb(nm + "_s", [128, 32], F32)
                    for nm in ("qs", "qf", "qbm")}
        for b in BB:
            for nm, tt in {**shared_gn, **shared_q}.items():
                setattr(b, nm, tt)
        shared_bf = {nm: sb(nm + "_s", [128, C], BF16)
                     for nm in ("attW", "z_fm", "y_bf", "kwk", "ln_fm",
                                "xd", "xd_fm")}
        shared_f32 = {nm: sb(nm + "_s", [128, C], F32)
                      for nm in ("junkA", "junkB", "ff_tm", "t0")}
        q8_s = sb("q8_s", [128, C], FP8)
        for b in BB:
            for nm, tt in shared_bf.items():
                setattr(b, nm, tt)
            for nm, tt in shared_f32.items():
                setattr(b, nm, tt)
            b.q8 = q8_s
            b.S_bf = b.d_fm
            b.G1 = b.mixA
            b.G2 = b.mixB
            b.ao = b.t0
            b.vq = b.junkA
            b.r_tm = b.ln_tm
            b.ysq = b.attW

        SHARED = ("attW", "z_fm", "y_bf", "kwk", "ln_fm", "xd", "xd_fm",
                  "junkA", "junkB", "ff_tm", "t0", "q8",
                  "gn_m", "gn_r", "gsum", "gsq", "gs1", "qs", "qf", "qbm")

        e_bd = BB[0].x2

        def N(b, nm):
            if nm in SHARED:
                return nm + "_s"
            return nm + b.p

        PPa = en(nc.psum_tensor("PPa", [128, 512], F32))
        PPb = en(nc.psum_tensor("PPb", [128, 512], F32))
        PAe = en(nc.psum_tensor("PAe", [128, 512], F32))
        PAo = en(nc.psum_tensor("PAo", [128, 512], F32))
        PY = en(nc.psum_tensor("PY", [128, 512], F32))
        PY2 = en(nc.psum_tensor("PY2", [128, 512], F32))
        PT = en(nc.psum_tensor("PT", [128, C], BF16))
        PT2 = en(nc.psum_tensor("PT2", [128, C], BF16))

        SINS = [en(nc.semaphore(f"sIN{i}")) for i in range(8)]
        SPDS = [en(nc.semaphore(f"sPD{i}")) for i in range(4)]
        SOUTS = [en(nc.semaphore(f"sOUT{i}")) for i in range(2)]
        sPE = en(nc.semaphore("sPE"))
        sDV = en(nc.semaphore("sDV"))
        sAC = en(nc.semaphore("sAC"))
        sPO = en(nc.semaphore("sPO"))
        sCC = en(nc.semaphore("sCC"))

        p.dma_pools = {"in": (SINS, [0]), "pd": (SPDS, [0]),
                       "out": (SOUTS, [0])}

        def _rot_dma(eng, key, dst, src, reads, writes, waits, **kw):
            return p.op(eng,
                        lambda g, a=dst, b=src: g.dma_start(out=a, in_=b,
                                                            **kw),
                        None, 16, reads, writes, waits, dma_kind=key,
                        cost=0.3)

        def dma(dst, src, reads=(), writes=(), waits=()):
            return _rot_dma("sync", "in", dst, src, reads, writes, waits)

        def dma_out(dst, src, reads=(), waits=()):
            return _rot_dma("sync", "out", dst, src, reads, (), waits)

        def pdma(dst, src, reads=(), writes=(), waits=(), **kw):
            return _rot_dma("gpsimd", "pd", dst, src, reads, writes, waits,
                            **kw)

        def dve(fn, reads=(), writes=(), waits=(), cost=None):
            return p.op("vector", fn, sDV, 1, reads, writes, waits,
                        cost=cost)

        def act(fn, reads=(), writes=(), waits=(), cost=None):
            return p.op("scalar", fn, sAC, 1, reads, writes, waits,
                        cost=cost)

        def pe(fn, reads=(), writes=(), waits=(), cost=None):
            return p.op("tensor", fn, sPE, 1, reads, writes, waits,
                        cost=cost)

        def pool(fn, reads=(), writes=(), waits=()):
            return p.op("gpsimd", fn, sPO, 1, reads, writes, waits)

        def dump(name, src_ap, reads):
            if name in dbg_dram:
                _rot_dma("gpsimd", "out", dbg_dram[name], src_ap, reads, (),
                         ())

        def transpose_set(src_sb, src_name, dst_sb, dst_name):
            for cc in range(8):
                P, ptn = ((PT, "PT"), (PT2, "PT2"))[cc % 2]
                pe(lambda t, c=cc, s=src_sb, P=P: t.transpose(
                    P[:, 128 * (c // 2):128 * (c // 2) + 128],
                    s[:, 128 * c:128 * c + 128], ident[:]),
                   reads=(src_name, "ident"), writes=(ptn,))
            dv = dst_sb.rearrange("p (c e q) -> p c e q", e=2, q=128)
            act(lambda s, d=dv: s.copy(
                out=d[:, :, 0, :],
                in_=PT.rearrange("p (c q) -> p c q", q=128)[:, 0:4, :]),
                reads=("PT",), writes=(dst_name,))
            tk = act(lambda s, d=dv: s.copy(
                out=d[:, :, 1, :],
                in_=PT2.rearrange("p (c q) -> p c q", q=128)[:, 0:4, :]),
                reads=("PT2",), writes=(dst_name,))
            p.chunk()
            return tk

        def dve_rsqrt_col(src, tmp, rcol, names_r, names_w):
            dve(lambda v: v.tensor_scalar(
                out=tmp.bitcast(I32), in0=src.bitcast(I32), scalar1=1,
                scalar2=None, op0=ALU.logical_shift_right),
                reads=names_r, writes=names_w, cost=0.2)
            dve(lambda v: v.tensor_scalar(
                out=rcol.bitcast(I32), in0=tmp.bitcast(I32), scalar1=-1,
                scalar2=0x5f3759df, op0=ALU.mult, op1=ALU.add),
                reads=names_r, writes=names_w, cost=0.2)
            dve(lambda v: v.tensor_tensor(out=tmp, in0=rcol, in1=rcol,
                                          op=ALU.mult),
                reads=names_r, writes=names_w, cost=0.2)
            dve(lambda v: v.scalar_tensor_tensor(
                out=tmp, in0=tmp, scalar=-0.5, in1=src, op0=ALU.mult,
                op1=ALU.mult),
                reads=names_r, writes=names_w, cost=0.2)
            dve(lambda v: v.scalar_tensor_tensor(
                out=rcol, in0=tmp, scalar=1.5, in1=rcol, op0=ALU.add,
                op1=ALU.mult),
                reads=names_r, writes=names_w, cost=0.2)

        def ln_stats(b, src_sb, src_name, st, st_name0, mcol, rcol, nmrcol,
                     eps, outname=None, dve_rs=False, jB=None, pre_sum=None):
            st_name = (st_name0, outname) if outname else (st_name0,)
            jb_ap, jb_nm = jB if jB else (b.junkB, N(b, "junkB"))
            if pre_sum is not None:
                dve(lambda v: v.scalar_tensor_tensor(
                    out=st[:, 0:1], in0=pre_sum[0], scalar=1.0,
                    in1=pre_sum[1], op0=ALU.mult, op1=ALU.add),
                    reads=st_name, writes=st_name, cost=0.2)
            else:
                dve(lambda v, s=src_sb: v.tensor_scalar(
                    out=s[:], in0=s[:], scalar1=1.0, scalar2=0.0,
                    op0=ALU.mult, op1=ALU.add, accum_out=st[:, 0:1]),
                    reads=(src_name,), writes=(src_name,) + st_name)
            act(lambda s, q=src_sb: s.activation(
                out=jb_ap[:], in_=q[:], func=AF.Square,
                accum_out=st[:, 1:2]),
                reads=(src_name,), writes=(jb_nm,) + st_name)
            p.chunk()
            dve(lambda v: v.tensor_scalar_mul(out=mcol, in0=st[:, 0:1],
                                              scalar1=1.0 / C),
                reads=st_name, writes=st_name, cost=0.2)
            dve(lambda v: v.tensor_scalar(
                out=st[:, 2:3], in0=mcol, scalar1=mcol, scalar2=float(eps),
                op0=ALU.mult, op1=ALU.subtract),
                reads=st_name, writes=st_name, cost=0.2)
            dve(lambda v: v.tensor_scalar(
                out=st[:, 3:4], in0=st[:, 1:2], scalar1=1.0 / C,
                scalar2=st[:, 2:3], op0=ALU.mult, op1=ALU.subtract),
                reads=st_name, writes=st_name, cost=0.2)
            if dve_rs:
                dve_rsqrt_col(st[:, 3:4], st[:, 4:5], rcol, st_name, st_name)
            else:
                act(lambda s: s.activation(out=st[:, 4:5], in_=st[:, 3:4],
                                           func=AF.Sqrt),
                    reads=st_name, writes=st_name)
                dve(lambda v: v.reciprocal(out=rcol, in_=st[:, 4:5]),
                    reads=st_name, writes=st_name)
            tok = dve(lambda v: v.tensor_scalar(
                out=nmrcol, in0=mcol, scalar1=rcol, scalar2=-1.0,
                op0=ALU.mult, op1=ALU.mult),
                reads=st_name, writes=st_name, cost=0.2)
            p.chunk()
            return tok

        def quantize(b, dst_sb, dst_name, src_sb, src_name, rstd_ap=None,
                     rstd_name=None):
            rd = (src_name,) if rstd_ap is None else (src_name, rstd_name)
            dve(lambda v, s=src_sb: v.tensor_reduce(
                out=b.qbm[:], in_=s.rearrange("p (a c) -> p a c", c=32),
                axis=AX.X, op=ALU.max, apply_absolute_value=True),
                reads=rd, writes=(N(b, "qbm"),))
            if rstd_ap is not None:
                dve(lambda v: v.tensor_scalar(
                    out=b.qs[:], in0=b.qbm[:], scalar1=rstd_ap,
                    scalar2=1.0 / 7.5, op0=ALU.mult, op1=ALU.mult),
                    reads=(N(b, "qbm"),) + ((rstd_name,) if rstd_name else ()),
                    writes=(N(b, "qs"),), cost=0.2)
            else:
                dve(lambda v: v.tensor_scalar_mul(out=b.qs[:], in0=b.qbm[:],
                                                  scalar1=1.0 / 7.5),
                    reads=(N(b, "qbm"),), writes=(N(b, "qs"),), cost=0.2)
            dve(lambda v: v.tensor_scalar_max(out=b.qs[:], in0=b.qs[:],
                                              scalar1=1e-30),
                reads=(N(b, "qs"),), writes=(N(b, "qs"),), cost=0.2)
            p.chunk()
            dve(lambda v: v.reciprocal(out=b.qf[:], in_=b.qs[:]),
                reads=(N(b, "qs"),), writes=(N(b, "qf"),), cost=0.3)
            if rstd_ap is not None:
                dve(lambda v: v.tensor_scalar(
                    out=b.qf[:], in0=b.qf[:], scalar1=rstd_ap, scalar2=0.0,
                    op0=ALU.mult, op1=ALU.add),
                    reads=(N(b, "qf"),) + ((rstd_name,) if rstd_name else ()),
                    writes=(N(b, "qf"),), cost=0.2)
            p.chunk()
            rb = b.qf[:].unsqueeze(2).broadcast_to([128, 32, 32])
            dve(lambda v, s=src_sb, r=rb: v.tensor_tensor(
                out=b.vq.rearrange("p (a c) -> p a c", c=32),
                in0=s.rearrange("p (a c) -> p a c", c=32), in1=r,
                op=ALU.mult),
                reads=(src_name, N(b, "qf")), writes=(N(b, "junkA"),))
            p.chunk()
            act(lambda s: s.activation(out=b.vq[:], in_=b.vq[:],
                                       func=AF.Copy, scale=8.0, bias=MAGIC),
                reads=(N(b, "junkA"),), writes=(N(b, "junkA"),))
            p.chunk()
            act(lambda s: s.activation(out=b.vq[:], in_=b.vq[:],
                                       func=AF.Copy, scale=0.125,
                                       bias=-MAGIC * 0.125),
                reads=(N(b, "junkA"),), writes=(N(b, "junkA"),))
            p.chunk()
            act(lambda s: s.copy(out=b.q8[:], in_=b.vq[:]),
                reads=(N(b, "junkA"),), writes=(N(b, "q8"),))
            p.chunk()
            sbb = b.qs[:].unsqueeze(2).broadcast_to([128, 32, 32])
            return dve(lambda v, d=dst_sb, r=sbb: v.tensor_tensor(
                out=d.rearrange("p (a c) -> p a c", c=32),
                in0=b.q8.rearrange("p (a c) -> p a c", c=32), in1=r,
                op=ALU.mult),
                reads=(N(b, "q8"), N(b, "qs")), writes=(dst_name,))

        def project(mix_sb, mix_name, w_sb, w_name, evac_list, init=None):
            if init is not None:
                for half, PP in ((0, PPa), (1, PPb)):
                    pn = "PPa" if half == 0 else "PPb"
                    pe(lambda t, h=half, P=PP: t.matmul(
                        P[:], ones1[0:1, :],
                        init[0][0:1, 512 * h:512 * h + 512],
                        start=True, stop=False),
                       reads=("ones1", init[1]), writes=(pn,))
            for kc in range(8):
                for half, PP in ((0, PPa), (1, PPb)):
                    pn = "PPa" if half == 0 else "PPb"
                    pe(lambda t, k=kc, h=half, P=PP, m=mix_sb, w=w_sb:
                       t.matmul(P[:], m[:, 128 * k:128 * k + 128],
                                w[:, C * k + 512 * h:C * k + 512 * h + 512],
                                start=(k == 0 and init is None), stop=(k == 7)),
                       reads=(mix_name, w_name), writes=(pn,))
            toks = []
            for half, PP in ((0, PPa), (1, PPb)):
                pn = "PPa" if half == 0 else "PPb"
                toks.append(evac_list[half](PP, pn))
            p.chunk()
            return toks

        b0, b1 = BB[0], BB[1]
        dma(ident[:], idin, writes=("ident",))
        dma(maa_sb[:], maa4, writes=("maa_sb",))
        dma(relay_d_sb[:], relay_d, writes=("relay_d_sb",))
        dma(relay_m_sb[:], relay_m, writes=("relay_m_sb",))
        pdma(bias_sb[:], bias_bc, writes=("bias_sb",))
        dma(b0.junkA[:], decay_bd, writes=("junkA_s",))
        act(lambda s: s.activation(out=e_bd[:], in_=b0.junkA[:], func=AF.Exp),
            reads=("junkA_s",), writes=("x20",))
        dve(lambda v: v.tensor_scalar_min(out=e_bd[:], in0=e_bd[:],
                                          scalar1=ECLAMP),
            reads=("x20",), writes=("x20",))
        act(lambda s: s.activation(
            out=ws_pair[:].unsqueeze(2),
            in_=e_bd.rearrange("p (b j) -> p b j", j=128)[:, :, 0:1],
            func=AF.Exp, scale=-128.0),
            reads=("x20",), writes=("ws_pair",))
        epv = e_bd.rearrange("p (b j) -> p b j", j=128)[:, :, 0]
        for r in range(4):
            dve(lambda v, rr=r, ep=epv: v.tensor_scalar(
                out=fac4[:, 8 * rr:8 * rr + 8], in0=ep,
                scalar1=relay_d_sb[:, rr:rr + 1], scalar2=0.0,
                op0=ALU.mult, op1=ALU.add),
                reads=("x20", "relay_d_sb"), writes=("fac4",))
            act(lambda s, rr=r: s.activation(
                out=fac4[:, 8 * rr:8 * rr + 8],
                in_=fac4[:, 8 * rr:8 * rr + 8], func=AF.Exp, scale=-2048.0),
                reads=("fac4",), writes=("fac4",))
            dve(lambda v, rr=r: v.tensor_scalar(
                out=fac4[:, 8 * rr:8 * rr + 8],
                in0=fac4[:, 8 * rr:8 * rr + 8],
                scalar1=relay_m_sb[:, rr:rr + 1], scalar2=0.0,
                op0=ALU.mult, op1=ALU.add),
                reads=("fac4", "relay_m_sb"), writes=("fac4",))
        dma(b0.junkB[:], qpat, writes=("junkB_s",))
        dve(lambda v: v.tensor_mul(out=b0.junkB[:], in0=b0.junkB[:],
                                   in1=e_bd[:]),
            reads=("junkB_s", "x20"), writes=("junkB_s",))
        act(lambda s: s.activation(out=wb_all[:], in_=b0.junkB[:],
                                   func=AF.Exp, scale=-1.0),
            reads=("junkB_s",), writes=("wb_all",))
        pdma(lnw_sb[:], lnw_bc, writes=("lnw_sb",))
        pdma(lnb_sb[:], lnb_bc, writes=("lnb_sb",))
        dma(b0.ff_tm[:], decay_hd, writes=("ff_tm_s",))
        dma(b0.stA[:, 0:1], iotarev, writes=("stA0",))
        act(lambda s: s.activation(out=b0.ff_tm[:], in_=b0.ff_tm[:],
                                   func=AF.Exp),
            reads=("ff_tm_s",), writes=("ff_tm_s",))
        dve(lambda v: v.tensor_scalar_min(out=b0.ff_tm[:], in0=b0.ff_tm[:],
                                          scalar1=ECLAMP),
            reads=("ff_tm_s",), writes=("ff_tm_s",))
        act(lambda s: s.activation(out=wk_tm[:], in_=b0.ff_tm[:], func=AF.Exp,
                                   scale=b0.stA[:, 0:1]),
            reads=("ff_tm_s", "stA0"), writes=("wk_tm",))
        for hf in range(2):
            cw = slice(C * hf, C * hf + C)
            dma(b0.junkA[:], decay_wm[:, cw], writes=("junkA_s",))
            act(lambda s: s.activation(out=b0.junkA[:], in_=b0.junkA[:],
                                       func=AF.Exp),
                reads=("junkA_s",), writes=("junkA_s",))
            dve(lambda v: v.tensor_scalar_min(out=b0.junkA[:],
                                              in0=b0.junkA[:],
                                              scalar1=ECLAMP),
                reads=("junkA_s",), writes=("junkA_s",))
            dma(b0.ff_tm[:], imj[:, cw], writes=("ff_tm_s",))
            dve(lambda v: v.tensor_mul(out=b0.junkA[:], in0=b0.junkA[:],
                                       in1=b0.ff_tm[:]),
                reads=("junkA_s", "ff_tm_s"), writes=("junkA_s",))
            act(lambda s: s.activation(out=b0.junkA[:], in_=b0.junkA[:],
                                       func=AF.Exp, scale=-1.0),
                reads=("junkA_s",), writes=("junkA_s",))
            dma(b0.ff_tm[:], maskio[:, cw], writes=("ff_tm_s",))
            dve(lambda v: v.tensor_mul(out=b0.junkA[:], in0=b0.junkA[:],
                                       in1=b0.ff_tm[:]),
                reads=("junkA_s", "ff_tm_s"), writes=("junkA_s",))
            dma(b0.ff_tm[:], udiag[:, cw], writes=("ff_tm_s",))
            dve(lambda v, h=hf: v.tensor_add(out=wm_b[:, C * h:C * h + C],
                                             in0=b0.junkA[:],
                                             in1=b0.ff_tm[:]),
                reads=("junkA_s", "ff_tm_s"), writes=("wm_b",))

        dve(lambda v: v.memset(S[:], 0.0), writes=("S",))
        dve(lambda v: v.memset(ones1[:], 1.0), writes=("ones1",))

        lnh = b0.ln_tm
        stH = b0.stA
        dma(b0.xbuf[0:1, :], xin[0:1, :], writes=("xbuf0",))
        dve(lambda v: v.tensor_scalar(
            out=b0.junkA[0:1, :], in0=b0.xbuf[0:1, :], scalar1=1.0,
            scalar2=0.0, op0=ALU.mult, op1=ALU.add,
            accum_out=stH[0:1, 0:1]),
            reads=("xbuf0",), writes=("junkA_s", "stA0"))
        act(lambda s: s.activation(out=b0.junkB[0:1, :], in_=b0.xbuf[0:1, :],
                                   func=AF.Square, accum_out=stH[0:1, 1:2]),
            reads=("xbuf0",), writes=("junkB_s", "stA0"))
        dve(lambda v: v.tensor_scalar_mul(out=stH[0:1, 2:3],
                                          in0=stH[0:1, 0:1],
                                          scalar1=1.0 / C),
            reads=("stA0",), writes=("stA0",))
        dve(lambda v: v.tensor_scalar(
            out=stH[0:1, 3:4], in0=stH[0:1, 2:3], scalar1=stH[0:1, 2:3],
            scalar2=EPS_LN, op0=ALU.mult, op1=ALU.subtract),
            reads=("stA0",), writes=("stA0",))
        dve(lambda v: v.tensor_scalar(
            out=stH[0:1, 4:5], in0=stH[0:1, 1:2], scalar1=1.0 / C,
            scalar2=stH[0:1, 3:4], op0=ALU.mult, op1=ALU.subtract),
            reads=("stA0",), writes=("stA0",))
        act(lambda s: s.activation(out=stH[0:1, 5:6], in_=stH[0:1, 4:5],
                                   func=AF.Sqrt),
            reads=("stA0",), writes=("stA0",))
        dve(lambda v: v.reciprocal(out=stH[0:1, 6:7], in_=stH[0:1, 5:6]),
            reads=("stA0",), writes=("stA0",))
        dve(lambda v: v.tensor_scalar(
            out=stH[0:1, 7:8], in0=stH[0:1, 2:3], scalar1=stH[0:1, 6:7],
            scalar2=-1.0, op0=ALU.mult, op1=ALU.mult),
            reads=("stA0",), writes=("stA0",))
        act(lambda s: s.activation(out=lnh[0:1, 0:C], in_=b0.xbuf[0:1, :],
                                   func=AF.Identity, scale=stH[0:1, 6:7],
                                   bias=stH[0:1, 7:8]),
            reads=("xbuf0", "stA0"), writes=("ln_tm0",))
        pdma(cc_in.ap()[0:1, :], lnh[0:1, 0:C],
             reads=("ln_tm0",), writes=("cc_in",))
        pdma(halo0[:],
             cc_in.ap()[0:1, :].rearrange("a (cc q) -> (a q) cc", q=128),
             reads=("cc_in",), writes=("halo0",),
             allow_slow_non_contiguous=True)

        def load_weight(dst_sb, dst_name, src):
            for cc in range(8):
                pdma(dst_sb[:, C * cc:C * cc + C],
                     src[128 * cc:128 * cc + 128, :], writes=(dst_name,))

        load_weight(WA, "WA", wdr["wkT"])
        load_weight(WB, "WB", wdr["wvT"])
        load_weight(WC, "WC", wdr["wrT"])
        load_weight(WD, "WD", wdr["wgT"])
        load_weight(WF, "WF", wdr["woT"])

        for ob in range(8):
            dma(b1.xbuf[:], wffn[128 * ob:128 * ob + 128, :],
                writes=("xbuf1",))
            quantize(b1, b1.xd, "xd_s", b1.xbuf, "xbuf1")
            for cc in range(8):
                P, ptn = ((PT, "PT"), (PT2, "PT2"))[cc % 2]
                pe(lambda t, c=cc, P=P: t.transpose(
                    P[:, 128 * (c // 2):128 * (c // 2) + 128],
                    b1.xd[:, 128 * c:128 * c + 128], ident[:]),
                   reads=("xd_s", "ident"), writes=(ptn,))
            wdv = WE.rearrange("p (cc e ob q) -> p cc e ob q", e=2, ob=8,
                               q=128)
            act(lambda s, o=ob, w=wdv: s.copy(
                out=w[:, :, 0, o, :],
                in_=PT.rearrange("p (cc q) -> p cc q", q=128)[:, 0:4, :]),
                reads=("PT",), writes=("WE",))
            act(lambda s, o=ob, w=wdv: s.copy(
                out=w[:, :, 1, o, :],
                in_=PT2.rearrange("p (cc q) -> p cc q", q=128)[:, 0:4, :]),
                reads=("PT2",), writes=("WE",))

        dump("wd", WE[:, 0:C], ("WE",))
        dump("wk_tm", wk_tm[:], ("wk_tm",))
        dump("halo0", halo0[:], ("halo0",))

        def x_load(b, t):
            dma(b.xbuf[:], xin[1 + TILE * t:1 + TILE * t + TILE, :],
                writes=(N(b, "xbuf"),))

        def ln_and_mixes(b, t, ca, cb, stats, ntmax):
            if t + 1 < ntmax:
                x_load(BB[(t + 1) % 2], t + 1)
            if stats:
                ln_stats(b, b.xbuf, N(b, "xbuf"), b.stA, N(b, "stA"),
                         m1[:, t:t + 1], rstd1[:, t:t + 1], nmr1[:, t:t + 1],
                         EPS_LN, outname="lnstats1")
            act(lambda s, tt=t: s.activation(
                out=b.ln_tm[:], in_=b.xbuf[:], func=AF.Identity,
                scale=rstd1[:, tt:tt + 1], bias=nmr1[:, tt:tt + 1]),
                reads=(N(b, "xbuf"), "lnstats1"), writes=(N(b, "ln_tm"),))
            transpose_set(b.ln_tm, N(b, "ln_tm"), b.ln_fm, N(b, "ln_fm"))
            lv = b.ln_fm.rearrange("p (cc q) -> p cc q", q=128)
            dv = b.d_fm.rearrange("p (cc q) -> p cc q", q=128)
            d_eng = dve if stats else pool
            d_eng(lambda g, lv=lv, dv=dv: g.tensor_tensor(
                out=dv[:, :, 1:128], in0=lv[:, :, 0:127],
                in1=lv[:, :, 1:128], op=ALU.subtract),
                reads=(N(b, "ln_fm"),), writes=(N(b, "d_fm"),))
            dve(lambda v, lv=lv, dv=dv: v.tensor_tensor(
                out=dv[:, :, 0:1], in0=halo[:].unsqueeze(2),
                in1=lv[:, :, 0:1], op=ALU.subtract),
                reads=(N(b, "ln_fm"), "halo"), writes=(N(b, "d_fm"),))
            dve(lambda v, lv=lv: v.tensor_copy(out=halo[:],
                                               in_=lv[:, :, 127]),
                reads=(N(b, "ln_fm"),), writes=("halo",))
            mix2(b, ca, cb, b.mixA, N(b, "mixA"), b.mixB, N(b, "mixB"))

        def mix2(b, ca, cb, mA, mAn, mB, mBn):
            dv3 = b.d_fm.rearrange("p (c q) -> p c q", q=128)
            maa_a = maa_sb[:, ca:ca + 8].unsqueeze(2).broadcast_to(
                [128, 8, 128])
            maa_b = maa_sb[:, cb:cb + 8].unsqueeze(2).broadcast_to(
                [128, 8, 128])
            dve(lambda v, o=mA.rearrange("p (c q) -> p c q", q=128),
                m=maa_a, i=dv3: v.tensor_tensor(out=o, in0=i, in1=m,
                                                op=ALU.mult),
                reads=(N(b, "d_fm"), "maa_sb"), writes=(mAn,))
            dve(lambda v, o=mA: v.tensor_add(out=o[:], in0=o[:],
                                             in1=b.ln_fm[:]),
                reads=(mAn, N(b, "ln_fm")), writes=(mAn,))
            p.chunk()
            dve(lambda v, o=mB.rearrange("p (c q) -> p c q", q=128),
                m=maa_b, i=dv3: v.tensor_tensor(out=o, in0=i, in1=m,
                                                op=ALU.mult),
                reads=(N(b, "d_fm"), "maa_sb"), writes=(mBn,))
            dve(lambda v, o=mB: v.tensor_add(out=o[:], in0=o[:],
                                             in1=b.ln_fm[:]),
                reads=(mBn, N(b, "ln_fm")), writes=(mBn,))
            p.chunk()

        def proj_to(b, mix, mix_name, w, w_name, dst, dst_name, gelu=False,
                    accum=None):
            def fn(s, P, lo, ac):
                if gelu:
                    return s.activation(out=dst[:, lo:lo + 512], in_=P[:],
                                        func=AF.Gelu)
                return s.activation(out=dst[:, lo:lo + 512], in_=P[:],
                                    func=AF.Copy,
                                    accum_out=(ac if accum else None))
            wr = (dst_name,) if accum is None else (dst_name, accum[1])
            project(mix, mix_name, w, w_name, [
                (lambda PP, pn, lo=0: act(
                    lambda s, P=PP, l=lo: fn(
                        s, P, l, accum[0][0] if accum else None),
                    reads=(pn,), writes=wr)),
                (lambda PP, pn, lo=512: act(
                    lambda s, P=PP, l=lo: fn(
                        s, P, l, accum[0][1] if accum else None),
                    reads=(pn,), writes=wr)),
            ])

        def kv_block(b):
            proj_to(b, b.mixA, N(b, "mixA"), WKCUR[0], WKCUR[1],
                    b.k_tm, N(b, "k_tm"))
            proj_to(b, b.mixB, N(b, "mixB"), WVCUR[0], WVCUR[1],
                    b.v_tm, N(b, "v_tm"))

        def attn_A_and_scan(b, use_pool=True):
            kwk_eng = pool if use_pool else dve
            kwk_eng(lambda g: g.tensor_mul(out=b.kwk[:], in0=b.k_tm[:],
                                           in1=wk_tm[:]),
                    reads=(N(b, "k_tm"), "wk_tm"), writes=(N(b, "kwk"),))
            for pb in range(8):
                PAx, pn = ((PAe, "PAe"), (PAo, "PAo"))[pb // 4]
                pe(lambda tt, bb=pb, P=PAx: tt.matmul(
                    P[:, 128 * (bb % 4):128 * (bb % 4) + 128],
                    b.kwk[:, 128 * bb:128 * bb + 128],
                    b.v_tm[:, 128 * bb:128 * bb + 128],
                    start=True, stop=True),
                   reads=(N(b, "kwk"), N(b, "v_tm")), writes=(pn,))
            p.chunk()
            S3 = S.rearrange("p (b q) -> p b q", q=128)
            wsb = ws_pair[:].unsqueeze(2).broadcast_to([128, 8, 128])
            dve(lambda v, o=S3, m=wsb: v.tensor_tensor(
                out=o, in0=o, in1=m, op=ALU.mult),
                reads=("S", "ws_pair"), writes=("S",))
            p.chunk()
            Se = S.rearrange("p (b two n) -> p b two n", two=2, n=64)
            for half, PAx, pn in ((0, PAe, "PAe"), (1, PAo, "PAo")):
                PAv = PAx.rearrange("p (b two n) -> p b two n", two=2, n=64)
                dve(lambda v, Se=Se, PAv=PAv, h4=4 * half:
                    v.scalar_tensor_tensor(
                        out=Se[0:64, h4:h4 + 4, 0, :],
                        in0=Se[0:64, h4:h4 + 4, 0, :], scalar=1.0,
                        in1=PAv[0:64, :, 0, :], op0=ALU.mult, op1=ALU.add),
                    reads=("S", pn), writes=("S",))
                dve(lambda v, Se=Se, PAv=PAv, h4=4 * half:
                    v.scalar_tensor_tensor(
                        out=Se[64:128, h4:h4 + 4, 1, :],
                        in0=Se[64:128, h4:h4 + 4, 1, :], scalar=1.0,
                        in1=PAv[64:128, :, 1, :], op0=ALU.mult, op1=ALU.add),
                    reads=("S", pn), writes=("S",))

        WKCUR = (WA, "WA")
        WVCUR = (WB, "WB")
        dve(lambda v: v.tensor_copy(out=halo[:], in_=halo0[:]),
            reads=("halo0",), writes=("halo",))
        x_load(BB[0], 0)

        for t in range(n_tiles + 2):
            stages = []
            if t < n_tiles:
                stages.append(lambda tt=t: ln_and_mixes(
                    BB[tt % 2], tt, 0, 8, stats=True, ntmax=n_tiles))
            if 0 <= t - 1 < n_tiles:
                stages.append(lambda tt=t: kv_block(BB[(tt - 1) % 2]))
            if t >= 2:
                stages.append(lambda tt=t: attn_A_and_scan(BB[(tt - 2) % 2],
                                                           use_pool=False))
            p.staged(*stages)

        dump("p1M", S[:], ("S",))

        if not phase2:
            p.barrier_wait("sync", [(s, p.count.get(s, 0)) for s in SOUTS])
            emit(nc, p)
            return nc

        def cc_trigger():
            pdma(cc_in.ap(), S[:], reads=("S",), writes=("cc_in",))
            p.op("gpsimd",
                 lambda g: g.collective_compute(
                     "AllGather", ALU.bypass,
                     replica_groups=[[0, 1, 2, 3], [4, 5, 6, 7]],
                     ins=[cc_in.ap().opt()], outs=[cc_out.ap().opt()]),
                 sCC, 1, reads=("cc_in",), writes=("cc_out",))
            p.chunk()

        def combine():
            S3 = S.rearrange("p (b q) -> p b q", q=128)
            for r in range(4):
                mb, mn = BB[r % 2].t0, N(BB[r % 2], "t0")
                pdma(mb[:], cc_out.ap()[128 * r:128 * r + 128, :],
                     reads=("cc_out",), writes=(mn,))
                fb = fac4[:, 8 * r:8 * r + 8].unsqueeze(2).broadcast_to(
                    [128, 8, 128])
                m3 = mb.rearrange("p (b q) -> p b q", q=128)
                if r == 0:
                    dve(lambda v, m=m3, f=fb: v.tensor_tensor(
                        out=S3, in0=m, in1=f, op=ALU.mult),
                        reads=(mn, "fac4"), writes=("S",))
                else:
                    dve(lambda v, m=m3, f=fb: v.tensor_tensor(
                        out=b0.junkB.rearrange("p (b q) -> p b q", q=128),
                        in0=m, in1=f, op=ALU.mult),
                        reads=(mn, "fac4"), writes=("junkB_s",))
                    dve(lambda v: v.tensor_add(out=S[:], in0=S[:],
                                               in1=b0.junkB[:]),
                        reads=("S", "junkB_s"), writes=("S",))
            dump("Sin", S[:], ("S",))



        dve(lambda v: v.tensor_copy(out=halo[:], in_=halo0[:]),
            reads=("halo0",), writes=("halo",))

        def p2_front(b, t):
            ln_and_mixes(b, t, 0, 8, stats=False, ntmax=NP2)
            kv_block(b)
            transpose_set(b.k_tm, N(b, "k_tm"), b.k_fm, N(b, "k_fm"))
            mix2(b, 16, 24, b.mixA, N(b, "mixA"), b.mixB, N(b, "mixB"))
            proj_to(b, b.mixA, N(b, "mixA"), WC, "WC", b.r_tm, N(b, "ln_tm"))
            proj_to(b, b.mixB, N(b, "mixB"), WD, "WD", b.g_tm, N(b, "g_tm"),
                    gelu=True)
            transpose_set(b.r_tm, N(b, "ln_tm"), b.r_fm, N(b, "r_fm"))
            pool(lambda g: g.tensor_mul(out=b.rwb[:], in0=b.r_fm[:],
                                        in1=wb_all[:]),
                 reads=(N(b, "r_fm"), "wb_all"), writes=(N(b, "rwb"),))

        def p2_mid(b, t):
            pool(lambda g: g.tensor_copy(out=b.S_bf[:], in_=S[:]),
                 reads=("S",), writes=(N(b, "d_fm"),))
            attv = b.attW.rearrange("p (b par q) -> p b par q", par=2, q=128)
            wmv = wm_b.rearrange("p (g b par q) -> p g b par q", g=2, par=2,
                                 q=128)
            for grp, PYx, pyn in ((0, PY, "PY"), (1, PY2, "PY2")):
                for i in range(4):
                    pb = 4 * grp + i
                    for par, PAx, pname in ((0, PAe, "PAe"), (1, PAo, "PAo")):
                        pe(lambda tt, bb=pb, pp=par, ii=i, P=PAx: tt.matmul(
                            P[:, 128 * ii:128 * ii + 128],
                            b.k_fm[64 * pp:64 * pp + 64,
                                   128 * bb:128 * bb + 128],
                            b.r_fm[64 * pp:64 * pp + 64,
                                   128 * bb:128 * bb + 128],
                            start=True, stop=True),
                           reads=(N(b, "k_fm"), N(b, "r_fm")),
                           writes=(pname,))
                for par, PAx, pname in ((0, PAe, "PAe"), (1, PAo, "PAo")):
                    dve(lambda v, g2=grp, pp=par, P=PAx, av=attv, wv=wmv:
                        v.tensor_tensor(
                            out=av[:, :, pp, :],
                            in0=P.rearrange("p (i q) -> p i q", q=128),
                            in1=wv[:, g2, :, pp, :], op=ALU.mult),
                        reads=(pname, "wm_b"), writes=("attW" + str(par),))
                for i in range(8):
                    h = 8 * grp + i
                    pe(lambda tt, hh=h, ii=i, P=PYx: tt.matmul(
                        P[:, 64 * ii:64 * ii + 64],
                        b.attW[:, 128 * ii:128 * ii + 128],
                        b.v_tm[:, 64 * hh:64 * hh + 64],
                        start=(ii == 0), stop=False),
                       reads=("attW" + str(i % 2), N(b, "v_tm")),
                       writes=(pyn,))
                for j in range(4):
                    pb = 4 * grp + j
                    pe(lambda tt, bb=pb, jj=j, P=PYx: tt.matmul(
                        P[:, 128 * jj:128 * jj + 128],
                        b.rwb[:, 128 * bb:128 * bb + 128],
                        b.S_bf[:, 128 * bb:128 * bb + 128],
                        start=False, stop=(jj == 3)),
                       reads=(N(b, "rwb"), N(b, "d_fm")), writes=(pyn,))
                act(lambda s, g2=grp, P=PYx: s.copy(
                    out=b.y_bf[:, 512 * g2:512 * g2 + 512], in_=P[:]),
                    reads=(pyn,), writes=(N(b, "y_bf"),))
                p.chunk()
            attn_A_and_scan(b)
            p.chunk()
            act(lambda s: s.activation(out=b.ysq[:], in_=b.y_bf[:],
                                       func=AF.Square),
                reads=(N(b, "y_bf"),), writes=("attW0", "attW1"))
            dve(lambda v: v.tensor_reduce(
                out=b.gsum[:], in_=b.y_bf.rearrange("p (h n) -> p h n", n=64),
                axis=AX.X, op=ALU.add),
                reads=(N(b, "y_bf"),), writes=(N(b, "gsum"),))
            dve(lambda v: v.tensor_reduce(
                out=b.gsq[:], in_=b.ysq.rearrange("p (h n) -> p h n", n=64),
                axis=AX.X, op=ALU.add),
                reads=("attW0", "attW1"), writes=(N(b, "gsq"),))
            p.chunk()
            dve(lambda v: v.tensor_scalar_mul(out=b.gn_m[:], in0=b.gsum[:],
                                              scalar1=1.0 / 64),
                reads=(N(b, "gsum"),), writes=(N(b, "gn_m"),), cost=0.2)
            act(lambda s: s.activation(out=b.gs1[:], in_=b.gn_m[:],
                                       func=AF.Square),
                reads=(N(b, "gn_m"),), writes=(N(b, "gs1"),), cost=0.3)
            dve(lambda v: v.scalar_tensor_tensor(
                out=b.gs1[:], in0=b.gsq[:], scalar=1.0 / 64, in1=b.gs1[:],
                op0=ALU.mult, op1=ALU.subtract),
                reads=(N(b, "gsq"), N(b, "gs1")), writes=(N(b, "gs1"),),
                cost=0.2)
            dve(lambda v: v.tensor_scalar_add(out=b.gs1[:], in0=b.gs1[:],
                                              scalar1=EPS_GN),
                reads=(N(b, "gs1"),), writes=(N(b, "gs1"),), cost=0.2)
            p.chunk()
            dve_rsqrt_col(b.gs1[:], b.gsum[:], b.gn_r[:],
                          (N(b, "gs1"), N(b, "gsum")),
                          (N(b, "gsum"), N(b, "gn_r")))
            p.chunk()
            pool(lambda g: g.tensor_mul(out=b.G1[:], in0=b.g_tm[:],
                                        in1=lnw_sb[:]),
                 reads=(N(b, "g_tm"), "lnw_sb"), writes=(N(b, "mixA"),))
            pool(lambda g: g.tensor_mul(out=b.G2[:], in0=b.g_tm[:],
                                        in1=lnb_sb[:]),
                 reads=(N(b, "g_tm"), "lnb_sb"), writes=(N(b, "mixB"),))
            gm_b = b.gn_m[:].unsqueeze(2).broadcast_to([128, H, 64])
            gr_b = b.gn_r[:].unsqueeze(2).broadcast_to([128, H, 64])
            y3 = b.y_bf.rearrange("p (h n) -> p h n", n=64)
            q3 = b.ysq.rearrange("p (h n) -> p h n", n=64)
            z3 = b.z1.rearrange("p (h n) -> p h n", n=64)
            dve(lambda v, y3=y3, q3=q3, m=gm_b: v.tensor_tensor(
                out=q3, in0=y3, in1=m, op=ALU.subtract),
                reads=(N(b, "y_bf"), N(b, "gn_m")), writes=("attW0", "attW1"))
            dve(lambda v, q3=q3, z3=z3, r=gr_b: v.tensor_tensor(
                out=z3, in0=q3, in1=r, op=ALU.mult),
                reads=("attW0", "attW1", N(b, "gn_r")),
                writes=(N(b, "z1"),))
            dve(lambda v: v.tensor_mul(out=b.ysq[:], in0=b.z1[:],
                                       in1=b.G1[:]),
                reads=(N(b, "z1"), N(b, "mixA")), writes=("attW0", "attW1"))
            dve(lambda v: v.tensor_add(out=b.z1[:], in0=b.ysq[:],
                                       in1=b.G2[:]),
                reads=("attW0", "attW1", N(b, "mixB")),
                writes=(N(b, "z1"),))
            p.chunk()

        def p2_tail_a(b, t):
            dma(b.junkB[:], xin[1 + TILE * t:1 + TILE * t + TILE, :],
                writes=(N(b, "junkB"),))
            p.chunk()
            transpose_set(b.z1, N(b, "z1"), b.z_fm, N(b, "z_fm"))
            proj_to(b, b.z_fm, N(b, "z_fm"), WF, "WF", b.ao, N(b, "t0"),
                    accum=((b.stW[:, 4:5], b.stW[:, 2:3]), N(b, "stW")))
            ln_stats(b, b.ao, N(b, "t0"), b.stW, N(b, "stW"), b.stW[:, 5:6],
                     b.stW[:, 6:7], b.stW[:, 7:8], EPS_LN, dve_rs=True,
                     jB=(b.x2, N(b, "x2")),
                     pre_sum=(b.stW[:, 4:5], b.stW[:, 2:3]))
            act(lambda s: s.activation(out=b.x2[:], in_=b.ao[:],
                                       func=AF.Identity,
                                       scale=b.stW[:, 6:7],
                                       bias=b.stW[:, 7:8]),
                reads=(N(b, "t0"), N(b, "stW")), writes=(N(b, "x2"),))
            dve(lambda v: v.tensor_add(out=b.x2[:], in0=b.junkB[:],
                                       in1=b.x2[:]),
                reads=(N(b, "junkB"), N(b, "x2")), writes=(N(b, "x2"),))
            p.chunk()

        def p2_tail_b(b, t):
            ln_stats(b, b.x2, N(b, "x2"), b.stX, N(b, "stX"), b.stX[:, 5:6],
                     b.stX[:, 6:7], b.stX[:, 7:8], EPS_LN, dve_rs=True,
                     jB=(b.junkA, N(b, "junkA")))
            dve(lambda v: v.tensor_scalar_mul(out=b.stX[:, 5:6],
                                              in0=b.stX[:, 5:6],
                                              scalar1=-1.0),
                reads=(N(b, "stX"),), writes=(N(b, "stX"),), cost=0.2)
            act(lambda s: s.activation(out=b.ff_tm[:], in_=b.x2[:],
                                       func=AF.Identity,
                                       bias=b.stX[:, 5:6]),
                reads=(N(b, "x2"), N(b, "stX")), writes=(N(b, "ff_tm"),))
            quantize(b, b.xd, N(b, "xd"), b.ff_tm, N(b, "ff_tm"),
                     rstd_ap=b.stX[:, 6:7], rstd_name=N(b, "stX"))
            p.chunk()
            transpose_set(b.xd, N(b, "xd"), b.xd_fm, N(b, "xd_fm"))
            project(b.xd_fm, N(b, "xd_fm"), WE, "WE", [
                (lambda PP, pn: act(
                    lambda s, P=PP: s.activation(
                        out=b.ff_tm[:, 0:512], in_=P[:], func=AF.Copy,
                        accum_out=b.stF[:, 4:5]),
                    reads=(pn,), writes=(N(b, "ff_tm"), N(b, "stF")))),
                (lambda PP, pn: act(
                    lambda s, P=PP: s.activation(
                        out=b.ff_tm[:, 512:1024], in_=P[:], func=AF.Copy,
                        accum_out=b.stF[:, 2:3]),
                    reads=(pn,), writes=(N(b, "ff_tm"), N(b, "stF")))),
            ], init=(bias_sb, "bias_sb"))
            ln_stats(b, b.ff_tm, N(b, "ff_tm"), b.stF, N(b, "stF"),
                     b.stF[:, 5:6], b.stF[:, 6:7], b.stF[:, 7:8], EPS_LN,
                     dve_rs=True, jB=(b.junkA, N(b, "junkA")),
                     pre_sum=(b.stF[:, 4:5], b.stF[:, 2:3]))
            act(lambda s: s.activation(out=b.junkA[:], in_=b.ff_tm[:],
                                       func=AF.Identity,
                                       scale=b.stF[:, 6:7],
                                       bias=b.stF[:, 7:8]),
                reads=(N(b, "ff_tm"), N(b, "stF")), writes=(N(b, "junkA"),))
            pool(lambda g: g.tensor_add(out=b.ff_tm[:], in0=b.x2[:],
                                        in1=b.junkA[:]),
                 reads=(N(b, "x2"), N(b, "junkA")), writes=(N(b, "ff_tm"),))
            dma_out(yout[TILE * t:TILE * t + TILE, :], b.ff_tm[:],
                    reads=(N(b, "ff_tm"),))

        NP2 = p2_tiles if p2_tiles is not None else n_tiles
        x_load(BB[0], 0)
        p.staged(cc_trigger, lambda: p2_front(BB[0], 0))
        if NP2 > 1:
            p.staged(lambda: p2_front(BB[1], 1))
        combine()
        p.staged(lambda: p2_mid(BB[0], 0))
        for s in range(2, NP2 + 3):
            stages = []
            if s < NP2:
                stages.append(lambda tt=s: p2_front(BB[tt % 2], tt))
            if s - 1 < NP2:
                stages.append(lambda tt=s: p2_mid(BB[(tt - 1) % 2], tt - 1))
            if 0 <= s - 2 < NP2:
                stages.append(
                    lambda tt=s: p2_tail_a(BB[(tt - 2) % 2], tt - 2))
            if 0 <= s - 3 < NP2:
                stages.append(
                    lambda tt=s: p2_tail_b(BB[(tt - 3) % 2], tt - 3))
            p.staged(*stages)

        p.barrier_wait("sync", [(s, p.count.get(s, 0)) for s in SOUTS])
        emit(nc, p)
    return nc


LAST_PROG = None


def emit(nc, p):
    global LAST_PROG
    LAST_PROG = p
    with nc.Block() as blk:
        blk.sync(p.runner("sync"))
        blk.tensor(p.runner("tensor"))
        blk.vector(p.runner("vector"))
        blk.scalar(p.runner("scalar"))
        blk.gpsimd(p.runner("gpsimd"))



def make_host_consts():
    c = {}
    q = np.arange(128, dtype=np.float32)
    c["qpat"] = np.tile(q, 8)[None, :].repeat(128, 0).astype(np.float32)
    j = np.arange(128, dtype=np.float32)[:, None]
    i = np.arange(128, dtype=np.float32)[None, :]
    c["imj"] = np.tile(np.maximum(i - j - 1.0, 0.0), (1, 16)).astype(
        np.float32)
    c["maskio"] = np.tile((i > j).astype(np.float32), (1, 16))
    c["iotarev"] = -(127.0 - np.arange(128, dtype=np.float32))[:, None].astype(
        np.float32)
    c["idin"] = np.eye(128).astype(ml_dtypes.bfloat16)
    return c


def make_in_maps(inp):
    return _make_in_maps(**inp)


def _make_in_maps(L0_attn_x, L0_attn_time_maa_k, L0_attn_time_maa_v,
                  L0_attn_time_maa_r, L0_attn_time_maa_g, L0_attn_time_decay,
                  L0_attn_time_faaaa, L0_attn_Wr, L0_attn_Wk, L0_attn_Wv,
                  L0_attn_Wg, L0_attn_Wo, L0_attn_ln_w, L0_attn_ln_b,
                  L0_ffn_weight, L0_ffn_bias):
    x = np.ascontiguousarray(np.asarray(L0_attn_x, dtype=np.float32))
    decay = np.asarray(L0_attn_time_decay, dtype=np.float32).reshape(H)
    faaaa = np.asarray(L0_attn_time_faaaa, dtype=np.float32).reshape(H)

    common = make_host_consts()
    common["wkT"] = np.ascontiguousarray(np.asarray(L0_attn_Wk).T)
    common["wvT"] = np.ascontiguousarray(np.asarray(L0_attn_Wv).T)
    common["wrT"] = np.ascontiguousarray(np.asarray(L0_attn_Wr).T)
    common["wgT"] = np.ascontiguousarray(np.asarray(L0_attn_Wg).T)
    common["woT"] = np.ascontiguousarray(np.asarray(L0_attn_Wo).T)
    common["wffn"] = np.ascontiguousarray(np.asarray(L0_ffn_weight))
    maa = np.zeros((128, 32), np.float32)
    for idx, m in enumerate((L0_attn_time_maa_k, L0_attn_time_maa_v,
                             L0_attn_time_maa_r, L0_attn_time_maa_g)):
        maa[:, 8 * idx:8 * idx + 8] = np.asarray(m).reshape(8, 128).T
    common["maa4"] = maa
    dbd = np.zeros((128, C), np.float32)
    for pb in range(8):
        dbd[0:64, 128 * pb:128 * pb + 128] = decay[2 * pb]
        dbd[64:128, 128 * pb:128 * pb + 128] = decay[2 * pb + 1]
    common["decay_bd"] = dbd
    common["decay_hd"] = np.repeat(decay, 64)[None, :].repeat(128, 0).astype(
        np.float32)
    common["decay_wm"] = np.repeat(decay, 128)[None, :].repeat(128, 0).astype(
        np.float32)
    ud = np.zeros((128, 2 * C), np.float32)
    for h in range(H):
        ud[:, 128 * h:128 * h + 128][np.arange(128), np.arange(128)] = \
            faaaa[h]
    common["udiag"] = ud
    common["lnw_bc"] = np.asarray(L0_attn_ln_w, np.float32)[None, :].repeat(
        128, 0)
    common["lnb_bc"] = np.asarray(L0_attn_ln_b, np.float32)[None, :].repeat(
        128, 0)
    common["bias_bc"] = np.asarray(L0_ffn_bias, np.float32)[None, :].repeat(
        128, 0)

    in_maps = []
    for core in range(8):
        bb, qp = core // 4, core % 4
        im = dict(common)
        xs = np.zeros((TOK + 1, C), np.float32)
        lo = qp * TOK
        xs[1:] = x[bb, lo:lo + TOK]
        if qp > 0:
            xs[0] = x[bb, lo - 1]
        im["x"] = xs
        rd = np.zeros((128, 4), np.float32)
        rm = np.zeros((128, 4), np.float32)
        for r in range(4):
            if r < qp:
                rd[:, r] = float(qp - 1 - r)
                rm[:, r] = 1.0
            else:
                rd[:, r] = 1e30
        im["relay_d"] = rd
        im["relay_m"] = rm
        in_maps.append(im)
    return in_maps


def kernel(**inputs):
    in_maps = make_in_maps(inputs)
    x = np.asarray(inputs["L0_attn_x"])
    B, T, _ = x.shape
    nc = build_nc()
    res = run_bass_kernel_spmd(nc, in_maps, core_ids=list(range(8)))
    out = np.zeros((B, T, C), np.float32)
    for core in range(8):
        bb, qp = core // 4, core % 4
        out[bb, qp * TOK:(qp + 1) * TOK] = res.results[core]["y"]
    return out


if __name__ == "__main__":
    nc = build_nc()
    print("built", nc.n_instructions(), "instructions")
